# version 27
# baseline (speedup 1.0000x reference)
import numpy as np
import ml_dtypes
from contextlib import ExitStack

import concourse.bass as bass
import concourse.mybir as mybir
from concourse.bass_utils import run_bass_kernel_spmd

F32 = mybir.dt.float32
BF16 = mybir.dt.bfloat16
AF = mybir.ActivationFunctionType
ALU = mybir.AluOpType
AX = mybir.AxisListType

B, S, C = 4, 4096, 1024
NH, DH, NG, GW = 16, 64, 32, 32
HALF = S // 2
NT = S // 128
NTO = HALF // 128
NC_ = C // 128
E, HID = 8, 1024
NKC = S // 128
SL = 512
NSL = HALF // SL
LN_EPS, GN_EPS = 1e-5, 1e-6
STAGE = 4


class Ctx:
    pass


def build_program(stage=STAGE):
    nc = bass.Bass("TRN2", target_bir_lowering=False)
    z = Ctx()
    z.nc = nc
    z.stage = stage

    z.x_h = nc.dram_tensor("x", [S, C], F32, kind="ExternalInput").ap()
    z.wqkv_h = nc.dram_tensor("wqkv", [C, 3 * C], F32, kind="ExternalInput").ap()
    z.bqkv_h = nc.dram_tensor("bqkv", [1, 3 * C], F32, kind="ExternalInput").ap()
    z.wo_h = nc.dram_tensor("wo", [C, C], F32, kind="ExternalInput").ap()
    z.bo_h = nc.dram_tensor("bo", [1, C], F32, kind="ExternalInput").ap()
    z.gam_h = nc.dram_tensor("gamma_cols", [128, NC_], F32, kind="ExternalInput").ap()
    z.bet_h = nc.dram_tensor("beta_cols", [128, NC_], F32, kind="ExternalInput").ap()
    z.wsi_h = nc.dram_tensor("w_sh_in", [C, 2 * HID], F32, kind="ExternalInput").ap()
    z.wso_h = nc.dram_tensor("w_sh_out", [HID, C], F32, kind="ExternalInput").ap()
    z.w1_h = nc.dram_tensor("w1_sh", [C, 2 * HID], F32, kind="ExternalInput").ap()
    z.w2_h = nc.dram_tensor("w2", [E, HID, C], F32, kind="ExternalInput").ap()
    z.wg_h = nc.dram_tensor("w_gate", [C, E], F32, kind="ExternalInput").ap()
    z.gb_h = nc.dram_tensor("gate_bias", [1, E], F32, kind="ExternalInput").ap()
    z.idf_h = nc.dram_tensor("identf", [128, 128], F32, kind="ExternalInput").ap()
    z.idb_h = nc.dram_tensor("identb", [128, 128], BF16, kind="ExternalInput").ap()
    z.ind4_h = nc.dram_tensor("ind4", [128, 4], F32, kind="ExternalInput").ap()
    z.ind4t_h = nc.dram_tensor("ind4t", [4, 128], F32, kind="ExternalInput").ap()
    z.onesf_h = nc.dram_tensor("onesf", [1, 128], F32, kind="ExternalInput").ap()
    z.out_h = nc.dram_tensor("out", [HALF, C], F32, kind="ExternalOutput").ap()
    z.resid_scr = nc.dram_tensor("resid_scr", [HALF, C], F32, kind="Internal").ap()
    z.attn_scr = nc.dram_tensor("attn_scr", [NTO * 64, HALF], BF16, kind="Internal").ap()
    z.xnfm_scr = nc.dram_tensor("xnfm_scr", [NSL * 128 * NC_, SL], BF16,
                                kind="Internal").ap()
    z.apj_scr = nc.dram_tensor("apj_scr", [HALF, C], BF16, kind="Internal").ap()

    with ExitStack() as top:
        z.top = top
        sbt = lambda name, shape, dt: top.enter_context(nc.sbuf_tensor(name, shape, dt))
        z.sbt = sbt
        z.mid = ExitStack()
        z.sbm = lambda name, shape, dt: z.mid.enter_context(nc.sbuf_tensor(name, shape, dt))

        z.idf = sbt("idf_s", [128, 128], F32)
        z.idb = sbt("idb_s", [128, 128], BF16)
        z.ind4 = sbt("ind4_s", [128, 4], F32)
        z.ind4t = sbt("ind4t_s", [4, 128], F32)
        z.onesf = sbt("onesf_s", [1, 128], F32)
        z.ones_bf = sbt("ones_bf", [128, 512], BF16)
        z.eps5 = sbt("eps5", [128, 1], F32)
        z.eps6 = sbt("eps6", [4, 1], F32)
        z.gam = sbt("gam", [128, NC_], F32)
        z.bet = sbt("bet", [128, NC_], F32)
        z.bo_bc = sbt("bo_bc", [128, C], F32)
        z.gb_bc = sbt("gb_bc", [128, E], F32)
        z.wgate = sbt("wgate", [128, NC_ * E], BF16)
        z.NM = sbt("NM", [128, NTO], F32)
        z.pre = sbt("pre", [128, NC_], F32)
        z.off = sbt("off", [128, NC_], F32)
        z.mcols = sbt("mcols", [128, NC_], F32)
        z.sm_nm = sbt("sm_nm", [128, 2], F32)
        z.gst = sbt("gst", [128, NC_], F32)
        z.g4 = sbt("g4", [4, 3 * NC_], F32)

        def sem(name):
            h = top.enter_context(nc.semaphore(name))
            return h
        z.brow = z.sbm("brow", [1, 3 * C], BF16)
        z.s_dx, z.dx = sem("s_dx"), 0
        z.s_dw, z.dw = sem("s_dw"), 0
        z.s_dsp, z.dsp = sem("s_dsp"), 0
        z.s_dsp2, z.dsp2 = sem("s_dsp2"), 0
        z.s_drl, z.drl = sem("s_drl"), 0
        z.s_dout, z.dout = sem("s_dout"), 0
        z.s_pe, z.pe = sem("s_pe"), 0
        z.s_pe2, z.pe2 = sem("s_pe2"), 0
        z.s_act, z.act = sem("s_act"), 0
        z.s_act2, z.act2 = sem("s_act2"), 0
        z.s_dve, z.dve = sem("s_dve"), 0
        z.s_dve2, z.dve2 = sem("s_dve2"), 0
        z.s_pool, z.pool = sem("s_pool"), 0

        _c0(z)
        _p0(z)
        if stage >= 1:
            _p1(z)
            _p2a(z)
        if stage >= 2:
            _attention(z)
            z.att_scope.close()
        if stage >= 3:
            _wo(z)
        z.mid.close()
        if stage >= 4:
            _ffn(z)
        else:
            _early_out(z)

        with nc.Block() as blk:
            @blk.sync
            def _(sy):
                sy.wait_ge(z.s_dout, z.dout)
    return nc


def _c0(z):
    nc = z.nc
    with nc.Block() as blk:
        @blk.sync
        def _(sy):
            for dst, src in [(z.idf, z.idf_h), (z.idb, z.idb_h), (z.ind4, z.ind4_h),
                             (z.ind4t, z.ind4t_h), (z.onesf, z.onesf_h),
                             (z.gam, z.gam_h), (z.bet, z.bet_h)]:
                sy.wait_ge(z.s_dx, z.dx)
                sy.dma_start(out=dst[:, :], in_=src).then_inc(z.s_dx, 16)
                z.dx += 16

        @blk.gpsimd
        def _(g):
            g.memset(z.ones_bf[:, :], 1.0)
            g.memset(z.eps5[:, :], LN_EPS)
            g.memset(z.eps6[:, :], GN_EPS)
            g.wait_ge(z.s_dw, z.dw)
            g.dma_start(out=z.brow[:, :], in_=z.bqkv_h).then_inc(z.s_dw, 16)
            z.dw += 16
            g.wait_ge(z.s_dw, z.dw)
            g.dma_start(out=z.wgate.rearrange("p (cc e) -> p cc e", e=E),
                        in_=z.wg_h.rearrange("(cc p) e -> p cc e", p=128)).then_inc(z.s_dw, 16)
            z.dw += 16


def _p0(z):
    nc = z.nc
    z.xn_fm = z.sbm("xn_fm", [128, NC_ * S], BF16)

    with ExitStack() as ps:
        z.xn_tm = ps.enter_context(nc.sbuf_tensor("xn_tm", [128, NT * C], BF16, side="right"))
        z.bo_f32 = ps.enter_context(nc.sbuf_tensor("bo_f32", [1, C], F32))
        z.gb_f32 = ps.enter_context(nc.sbuf_tensor("gb_f32", [1, E], F32))
        x_t = [ps.enter_context(nc.sbuf_tensor(f"x_t{i}", [128, 4 * C], F32)) for i in range(2)]
        rs_t = [ps.enter_context(nc.sbuf_tensor(f"rs_t{i}", [128, 2 * C], F32)) for i in range(2)]
        nmean = ps.enter_context(nc.sbuf_tensor("nmean", [128, 4], F32))
        sumsq = ps.enter_context(nc.sbuf_tensor("sumsq", [128, 4], F32))
        rstd = ps.enter_context(nc.sbuf_tensor("rstd", [128, 4], F32))
        m2c = ps.enter_context(nc.sbuf_tensor("m2c", [128, 4], F32))
        varc = ps.enter_context(nc.sbuf_tensor("varc", [128, 4], F32))
        sq_ps = ps.enter_context(nc.psum_tensor("sq_ps", [128, C], F32))
        tr_ps = [ps.enter_context(nc.psum_tensor(f"tr_ps{i}", [128, C], F32)) for i in range(3)]

        b_dx, b_dve, b_dve2, b_act, b_dsp2 = z.dx, z.dve, z.dve2, z.act, z.dsp2
        b_pe, b_act2, b_pool = z.pe, z.act2, z.pool
        c_red, c_var, c_rec = {}, {}, {}
        c_sq, c_mul, c_m2, c_sqrt = {}, {}, {}, {}
        c_ts, c_add = {}, {}
        dv = b_dve
        for t in range(NT):
            dv += 1; c_red[t] = dv
            if t >= 1:
                dv += 1; c_var[t - 1] = dv
                dv += 1; c_rec[t - 1] = dv
        dv += 1; c_var[NT - 1] = dv
        dv += 1; c_rec[NT - 1] = dv
        da = b_act
        for t in range(NT):
            da += 1; c_sq[t] = da
            da += 1; c_mul[t] = da
            da += 1; c_m2[t] = da
            if t >= 1:
                da += 1; c_sqrt[t - 1] = da
        da += 1; c_sqrt[NT - 1] = da
        dp = b_pool
        for t in range(NT):
            dp += 1; c_ts[t] = dp
            if 1 <= t <= NTO:
                dp += 1; c_add[t - 1] = dp

        with nc.Block() as blk:
            @blk.sync
            def _(sy):
                for dst, srch in [(z.bo_f32, z.bo_h), (z.gb_f32, z.gb_h)]:
                    sy.wait_ge(z.s_dx, z.dx)
                    sy.dma_start(out=dst[:, :], in_=srch).then_inc(z.s_dx, 16)
                    z.dx += 16
                for g in range(NT // 4):
                    p = g % 2
                    if g >= 2:
                        L = 4 * g - 5
                        sy.wait_ge(z.s_dve, c_red[L])
                        sy.wait_ge(z.s_act, c_sq[L])
                        sy.wait_ge(z.s_pool, c_ts[L] if L >= NTO else c_add[L])
                        if L >= NTO:
                            sy.wait_ge(z.s_dve2, b_dve2 + L + 1)
                    sy.wait_ge(z.s_dx, z.dx)
                    sy.dma_start(out=x_t[p][:, :].rearrange("p (two c) -> p two c", c=C),
                                 in_=z.x_h[g * 512:(g + 1) * 512, :].rearrange(
                                     "(two p) c -> p two c", p=128)).then_inc(z.s_dx, 16)
                    z.dx += 16

            @blk.gpsimd
            def _(gq):
                def xs_rr(t):
                    g, sb = t // 4, t % 4
                    xs = x_t[g % 2][:, sb * C:(sb + 1) * C]
                    g2, sb2 = t // 2, t % 2
                    rr = rs_t[g2 % 2][:, sb2 * C:(sb2 + 1) * C] if t < NTO else xs
                    return xs, rr, g2, sb2, sb
                for t in range(NT):
                    xs, rr, g2, sb2, sb = xs_rr(t)
                    gq.wait_ge(z.s_dve, c_rec[t])
                    if t < NTO and g2 >= 2 and sb2 == 0:
                        gq.wait_ge(z.s_dsp2, b_dsp2 + (g2 - 1) * 16)
                    gq.tensor_scalar(rr, xs, nmean[:, sb:sb + 1],
                                     rstd[:, sb:sb + 1], ALU.add, ALU.mult
                                     ).then_inc(z.s_pool, 1)
                    if 1 <= t <= NTO:
                        ta = t - 1
                        xs2, rr2, g2b, sb2b, _ = xs_rr(ta)
                        gq.wait_ge(z.s_dve2, b_dve2 + ta + 1)
                        gq.wait_ge(z.s_pool, c_ts[ta])
                        gq.tensor_add(rr2, rr2, xs2).then_inc(z.s_pool, 1)
                        if t % 2 == 0:
                            gsp = (t - 2) // 2
                            gq.wait_ge(z.s_pool, c_add[t - 1])
                            gq.wait_ge(z.s_dsp2, z.dsp2)
                            gq.dma_start(out=z.resid_scr[gsp * 256:(gsp + 1) * 256, :].rearrange(
                                             "(two p) c -> p two c", p=128),
                                         in_=rs_t[gsp % 2][:, :].rearrange(
                                             "p (two c) -> p two c", c=C)
                                         ).then_inc(z.s_dsp2, 16)
                            z.dsp2 += 16
                z.pool = dp

            @blk.vector
            def _(v):
                def emit_var(t):
                    sb = t % 4
                    v.wait_ge(z.s_act, c_m2[t])
                    v.tensor_scalar(varc[:, sb:sb + 1], sumsq[:, sb:sb + 1], 1.0 / C,
                                    m2c[:, sb:sb + 1], ALU.mult, ALU.subtract
                                    ).then_inc(z.s_dve, 1)

                def emit_rec(t):
                    sb = t % 4
                    v.wait_ge(z.s_act, c_sqrt[t])
                    v.reciprocal(rstd[:, sb:sb + 1], rstd[:, sb:sb + 1]).then_inc(z.s_dve, 1)

                def emit_copy(t):
                    g, sb = t // 4, t % 4
                    xs = x_t[g % 2][:, sb * C:(sb + 1) * C]
                    g2, sb2 = t // 2, t % 2
                    rr = rs_t[g2 % 2][:, sb2 * C:(sb2 + 1) * C] if t < NTO else xs
                    v.wait_ge(z.s_pool, c_ts[t])
                    v.tensor_copy(z.xn_tm[:, t * C:(t + 1) * C], rr).then_inc(z.s_dve2, 1)

                for t in range(NT):
                    g, sb = t // 4, t % 4
                    xs = x_t[g % 2][:, sb * C:(sb + 1) * C]
                    v.wait_ge(z.s_dx, b_dx + 32 + (g + 1) * 16)
                    v.tensor_reduce(nmean[:, sb:sb + 1], xs, AX.X, ALU.add,
                                    negate=True).then_inc(z.s_dve, 1)
                    if t >= 1:
                        emit_var(t - 1)
                        emit_rec(t - 1)
                    if t >= 2:
                        emit_copy(t - 2)
                emit_var(NT - 1)
                emit_rec(NT - 1)
                emit_copy(NT - 2)
                emit_copy(NT - 1)
                z.dve = dv
                z.dve2 = b_dve2 + NT

            @blk.scalar
            def _(s):
                def emit_evac(tt):
                    s.wait_ge(z.s_pe, b_pe + tt + 1)
                    s.copy(z.xn_fm.rearrange("p (fc ss) -> p fc ss", ss=S)
                           [:, :, tt * 128:(tt + 1) * 128],
                           tr_ps[tt % 3][:, :].rearrange("p (fc n) -> p fc n", n=128)
                           ).then_inc(z.s_act2, 1)

                for t in range(NT):
                    g, sb = t // 4, t % 4
                    xs = x_t[g % 2][:, sb * C:(sb + 1) * C]
                    s.wait_ge(z.s_dx, b_dx + 32 + (g + 1) * 16)
                    s.activation(sq_ps[:, :], xs, AF.Square, scale=1.0,
                                 accum_out=sumsq[:, sb:sb + 1]).then_inc(z.s_act, 1)
                    s.wait_ge(z.s_dve, c_red[t])
                    s.mul(nmean[:, sb:sb + 1], nmean[:, sb:sb + 1], 1.0 / C).then_inc(z.s_act, 1)
                    s.wait_ge(z.s_act, c_mul[t])
                    s.activation(m2c[:, sb:sb + 1], nmean[:, sb:sb + 1], AF.Square,
                                 scale=1.0).then_inc(z.s_act, 1)
                    if t >= 1:
                        sbp = (t - 1) % 4
                        s.wait_ge(z.s_dve, c_var[t - 1])
                        s.activation(rstd[:, sbp:sbp + 1], varc[:, sbp:sbp + 1], AF.Sqrt,
                                     bias=z.eps5[:, :], scale=1.0).then_inc(z.s_act, 1)
                    if t >= 2:
                        emit_evac(t - 2)
                sbp = (NT - 1) % 4
                s.wait_ge(z.s_dve, c_var[NT - 1])
                s.activation(rstd[:, sbp:sbp + 1], varc[:, sbp:sbp + 1], AF.Sqrt,
                             bias=z.eps5[:, :], scale=1.0).then_inc(z.s_act, 1)
                emit_evac(NT - 2)
                emit_evac(NT - 1)
                z.act = da
                z.act2 = b_act2 + NT

            @blk.tensor
            def _(t):
                t.wait_ge(z.s_dx, b_dx + 16)
                for tt in range(NT):
                    t.wait_ge(z.s_dve2, b_dve2 + tt + 1)
                    if tt >= 3:
                        t.wait_ge(z.s_act2, b_act2 + tt - 2)
                    for fc in range(NC_):
                        mm = t.matmul(tr_ps[tt % 3][:, fc * 128:(fc + 1) * 128],
                                      z.xn_tm[:, tt * C + fc * 128: tt * C + fc * 128 + 128],
                                      z.idb[:, :], start=True, stop=True)
                    mm.then_inc(z.s_pe, 1)
                    z.pe += 1
                t.wait_ge(z.s_act2, b_act2 + NT)
                for jj in range(2):
                    t.matmul(tr_ps[0][:, jj * 512:(jj + 1) * 512], z.onesf[:, :],
                             z.bo_f32[0:1, jj * 512:(jj + 1) * 512], start=True, stop=True)
                t.matmul(tr_ps[1][:, 0:8], z.onesf[:, :], z.gb_f32[:, :],
                         start=True, stop=True).then_inc(z.s_pe, 1)
                z.pe += 1

            @blk.vector
            def _(v2):
                v2.wait_ge(z.s_pe, b_pe + NT + 1)
                v2.tensor_copy(z.bo_bc[:, :], tr_ps[0][:, :])
                v2.tensor_copy(z.gb_bc[:, :], tr_ps[1][:, 0:8])


def _p1(z):
    nc = z.nc
    with ExitStack() as ps:
        g_ps = ps.enter_context(nc.psum_tensor("g_ps", [4, 2 * NC_], F32))
        e_ps = ps.enter_context(nc.psum_tensor("e_ps", [128, 2 * NC_], F32))
        sqo = ps.enter_context(nc.sbuf_tensor("sqo", [128, S], BF16))
        gst2 = ps.enter_context(nc.sbuf_tensor("gst2", [128, NC_], F32))
        m2g = ps.enter_context(nc.sbuf_tensor("m2g", [4, NC_], F32))
        varg = ps.enter_context(nc.sbuf_tensor("varg", [4, NC_], F32))

        b_dve, b_act, b_pe = z.dve, z.act, z.pe
        kinv = 1.0 / (S * GW)
        with nc.Block() as blk:
            @blk.vector
            def _(v):
                dv = b_dve
                for fc in range(NC_):
                    v.tensor_reduce(z.gst[:, fc:fc + 1], z.xn_fm[:, fc * S:(fc + 1) * S],
                                    AX.X, ALU.add).then_inc(z.s_dve, 1)
                    dv += 1
                v.wait_ge(z.s_pe, b_pe + 2)
                v.wait_ge(z.s_act, b_act + 10)
                v.scalar_tensor_tensor(varg[:, :], g_ps[0:4, NC_:2 * NC_], kinv,
                                       m2g[:, :], ALU.mult, ALU.subtract
                                       ).then_inc(z.s_dve, 1)
                dv += 1
                v.wait_ge(z.s_act, b_act + 11)
                v.reciprocal(z.g4[:, 2 * NC_:3 * NC_],
                             z.g4[:, NC_:2 * NC_]).then_inc(z.s_dve, 1)
                dv += 1
                v.wait_ge(z.s_pe, b_pe + 4)
                v.tensor_mul(z.pre[:, :], e_ps[:, NC_:2 * NC_],
                             z.gam[:, :]).then_inc(z.s_dve, 1)
                dv += 1
                v.wait_ge(z.s_dve, dv)
                v.wait_ge(z.s_act, b_act + 12)
                v.tensor_mul(z.off[:, :], z.mcols[:, :], z.pre[:, :]).then_inc(z.s_dve, 1)
                dv += 1
                v.wait_ge(z.s_dve, dv)
                v.tensor_add(z.off[:, :], z.off[:, :], z.bet[:, :]).then_inc(z.s_dve, 1)
                dv += 1
                z.dve = dv

            @blk.scalar
            def _(s):
                da = b_act
                s.wait_ge(z.s_dsp2, z.dsp2)
                for fc in range(NC_):
                    if fc > 0:
                        s.wait_ge(z.s_act, da)
                    s.activation(sqo[:, :], z.xn_fm[:, fc * S:(fc + 1) * S], AF.Square,
                                 scale=1.0, accum_out=gst2[:, fc:fc + 1]).then_inc(z.s_act, 1)
                    da += 1
                s.wait_ge(z.s_pe, b_pe + 2)
                s.mul(z.g4[:, 0:NC_], g_ps[0:4, 0:NC_], -kinv).then_inc(z.s_act, 1)
                da += 1
                s.wait_ge(z.s_act, da)
                s.activation(m2g[:, :], z.g4[0:4, 0:NC_], AF.Square,
                             scale=1.0).then_inc(z.s_act, 1)
                da += 1
                s.wait_ge(z.s_dve, b_dve + NC_ + 1)
                s.activation(z.g4[:, NC_:2 * NC_], varg[0:4, :], AF.Sqrt,
                             bias=z.eps6[:, :], scale=1.0).then_inc(z.s_act, 1)
                da += 1
                s.wait_ge(z.s_pe, b_pe + 4)
                s.copy(z.mcols[:, :], e_ps[:, 0:NC_]).then_inc(z.s_act, 1)
                da += 1
                z.act = da

            @blk.tensor
            def _(t):
                t.wait_ge(z.s_dve, b_dve + NC_)
                t.matmul(g_ps[:, 0:NC_], z.ind4[:, :], z.gst[:, :],
                         start=True, stop=True).then_inc(z.s_pe, 1)
                z.pe += 1
                t.wait_ge(z.s_act, b_act + NC_)
                t.matmul(g_ps[:, NC_:2 * NC_], z.ind4[:, :], gst2[:, :],
                         start=True, stop=True).then_inc(z.s_pe, 1)
                z.pe += 1
                t.wait_ge(z.s_act, b_act + NC_ + 1)
                t.matmul(e_ps[:, 0:NC_], z.ind4t[:, :], z.g4[0:4, 0:NC_],
                         start=True, stop=True).then_inc(z.s_pe, 1)
                z.pe += 1
                t.wait_ge(z.s_dve, b_dve + NC_ + 2)
                t.matmul(e_ps[:, NC_:2 * NC_], z.ind4t[:, :], z.g4[0:4, 2 * NC_:3 * NC_],
                         start=True, stop=True).then_inc(z.s_pe, 1)
                z.pe += 1


def _p2a(z):
    nc = z.nc
    with nc.Block() as blk:
        @blk.sync
        def _(sy):
            for fc in range(NC_):
                sy.wait_ge(z.s_dsp, z.dsp)
                sy.dma_start(out=z.xnfm_scr.rearrange("(sq p cc) n -> p sq cc n",
                                                       p=128, cc=NC_)[:, :, fc, :],
                             in_=z.xn_fm[:, fc * S: fc * S + HALF].rearrange(
                                 "p (sq n) -> p sq n", n=SL)).then_inc(z.s_dsp, 16)
                z.dsp += 16
            sy.wait_ge(z.s_dsp, z.dsp)

    with nc.Block() as blk:
        @blk.vector
        def _(v):
            for fc in range(NC_):
                v.tensor_scalar(z.xn_fm[:, fc * S:(fc + 1) * S],
                                z.xn_fm[:, fc * S:(fc + 1) * S],
                                z.pre[:, fc:fc + 1], z.off[:, fc:fc + 1],
                                ALU.mult, ALU.add)
    z.h_fm = z.xn_fm


def _attention(z):
    nc = z.nc
    z.wo_sb = z.sbm("wo_sb", [64, NH * C], BF16)
    z.att_scope = ExitStack()
    sba = lambda name, shape, dt: z.att_scope.enter_context(nc.sbuf_tensor(name, shape, dt))
    z.att_t = [sba(f"att_t{i}", [64, HALF], BF16) for i in range(2)]
    vgrp = sba("vgrp", [128, NT * 8 * 65], BF16)
    kbuf = [sba(f"kp{i}", [65, S], BF16) for i in range(2)]
    qbuf = [sba(f"qp{i}", [65, HALF], BF16) for i in range(2)]
    ptb = [sba(f"pt{i}", [128, 1024], BF16) for i in range(2)]
    wkq = [sba(f"wkq{i}", [128, NC_ * 128], BF16) for i in range(2)]
    wv = [sba("wv0", [128, NC_ * 512], BF16)]
    wv.append(wv[0])
    srow = sba("srow", [1, 1024], F32)
    rec_bc = sba("rec_bc", [64, 1024], F32)
    z.ks_sb = sba("ks_sb", [64, 512], BF16)
    z.smn = sba("smn", [128, 8], F32)

    sem = lambda name: z.att_scope.enter_context(nc.semaphore(name))
    for nm in ("p_kq", "p_sa", "p_sr", "p_st", "p_ctx", "p_bc",
               "a_kq", "a_sr", "a_ex", "a_nc", "a_bc",
               "d_rd", "d_nm", "d_mu"):
        setattr(z, nm, sem(nm))
        setattr(z, "c_" + nm, 0)

    wv_done = [0, 0]
    z.wkq_cnt = {}

    with nc.Block() as blk:
        @blk.gpsimd
        def _(g):
            g.wait_ge(z.s_dsp2, z.dsp2)
            g.memset(vgrp.rearrange("p (t h o) -> p t h o", h=8, o=65)[:, :, :, 64:65], 1.0)
            g.memset(kbuf[0][64:65, :], 1.0)
            g.memset(kbuf[1][64:65, :], 1.0)
            g.wait_ge(z.s_dw, z.dw)
            g.dma_start(out=wv[0].rearrange("p (cc n) -> p cc n", n=512),
                        in_=z.wqkv_h[:, 2 * C:2 * C + 512].rearrange("(cc p) n -> p cc n", p=128)
                        ).then_inc(z.s_dw, 16)
            z.dw += 16
            wv_done[0] = z.dw
            for hh0 in range(2):
                for half_ in range(2):
                    g.wait_ge(z.s_dw, z.dw)
                    g.dma_start(out=wkq[hh0].rearrange("p (cc n) -> p cc n", n=128)
                                [:, :, half_ * 64:half_ * 64 + 64],
                                in_=z.wqkv_h[:, half_ * C + hh0 * 64: half_ * C + hh0 * 64 + 64]
                                .rearrange("(cc p) n -> p cc n", p=128)).then_inc(z.s_dw, 16)
                    z.dw += 16
                z.wkq_cnt[hh0] = z.dw

    for grp in range(2):
        with ExitStack() as vs:
            v_ps = [vs.enter_context(nc.psum_tensor(f"v_ps{grp}{i}", [128, 512], F32))
                    for i in range(2)]
            b_pe, b_act = z.pe, z.act
            with nc.Block() as blk:
                @blk.tensor
                def _(t):
                    t.wait_ge(z.s_dw, wv_done[grp])
                    t.wait_ge(z.s_dw, 16)
                    for tt in range(NT):
                        p = tt % 2
                        if tt >= 2:
                            t.wait_ge(z.s_act, b_act + tt - 1)
                        for cc in range(NC_):
                            t.matmul(v_ps[p][:, :],
                                     z.h_fm[:, cc * S + tt * 128: cc * S + (tt + 1) * 128],
                                     wv[grp][:, cc * 512:(cc + 1) * 512],
                                     start=(cc == 0), stop=False)
                        t.matmul(v_ps[p][:, :], z.ones_bf[0:1, 0:128],
                                 z.brow[0:1, 2 * C + grp * 512: 2 * C + grp * 512 + 512],
                                 start=False, stop=True).then_inc(z.s_pe, 1)
                        z.pe += 1

                @blk.scalar
                def _(s):
                    for tt in range(NT):
                        p = tt % 2
                        s.wait_ge(z.s_pe, b_pe + tt + 1)
                        dst = vgrp[:, tt * 520: tt * 520 + 520].rearrange(
                            "p (h o) -> p h o", o=65)[:, :, 0:64]
                        s.copy(dst, v_ps[p][:, :].rearrange("p (h d) -> p h d", d=64)
                               ).then_inc(z.s_act, 1)
                        z.act += 1

                @blk.gpsimd
                def _(g):
                    if grp == 1:
                        g.wait_ge(z.s_dw, z.dw)
                        g.dma_start(out=z.wo_sb.rearrange("d (h n) -> d h n", n=C),
                                    in_=z.wo_h.rearrange("(h d) n -> d h n", d=64)
                                    ).then_inc(z.s_dw, 16)
                        z.dw += 16
                        z.wo_done = z.dw

        _att_group(z, grp, kbuf, qbuf, wkq, vgrp, ptb, srow, rec_bc, wv, wv_done)


def _att_group(z, grp, kbuf, qbuf, wkq, vgrp, ptb, srow, rec_bc, wv, wv_done):
    nc = z.nc
    NHH = 8
    B = {nm: getattr(z, "c_" + nm) for nm in
         ("p_kq", "p_sa", "p_sr", "p_st", "p_ctx", "p_bc",
          "a_kq", "a_sr", "a_ex", "a_nc", "a_bc", "d_rd", "d_nm", "d_mu")}
    b_dsp = z.dsp
    dw0 = z.dw

    pf_targets = []
    for j in range(NHH):
        tgt = grp * 8 + j + 2
        if tgt < NH:
            pf_targets.append((j, tgt))
    cnt = dw0 + (16 if grp == 0 else 0)
    for (j, tgt) in pf_targets:
        cnt += 32
        z.wkq_cnt[tgt] = cnt
    pf_set = {j for (j, _) in pf_targets}


    def a_items(j):
        its = []
        if j < NHH:
            if j in pf_set:
                its.append(("pf", j))
            its += [("kq", j, g) for g in range(12)]
            its.append(("ks", j))
            for qt in range(NTO):
                its.append(("pa", j, qt))
                its.append(("sr", j, qt))
        return its

    def b_items(j):
        its = []
        for qh in range(2):
            for k in range(NKC):
                its.append(("st", j, qh, k))
                if k >= 1:
                    its.append(("cx", j, qh, k - 1))
            its.append(("cx", j, qh, NKC - 1))
            its.append(("nc", j, qh))
            if qh == 0:
                its += [("bc", j, 0, i) for i in range(2)]
                its.append(("mu", j, 0))
        return its

    regions = []
    for r in range(NHH + 2):
        prefix = []
        if r >= 2:
            prefix = [("bc", r - 2, 1, i) for i in range(2)] + [("mu", r - 2, 1)]
        A = a_items(r)
        Bl = b_items(r - 1) if 1 <= r <= NHH else []
        if not Bl:
            merged = prefix + A
        else:
            head_b, rest_b = Bl[:1], Bl[1:]
            merged = list(head_b) + prefix
            ai = 0
            for bi, bit in enumerate(rest_b):
                merged.append(bit)
                want = (bi + 1) * len(A) // len(rest_b)
                while ai < want:
                    merged.append(A[ai])
                    ai += 1
            merged += A[ai:]
        regions.append(merged)
    sched = [it for reg in regions for it in reg]
    pre_sr, pre_bc = {}, {}
    nsr = nbc = 0
    for i, it in enumerate(sched):
        if it[0] == "sr":
            pre_sr[i] = nsr; pre_bc[i] = nbc; nsr += 1
        elif it[0] == "bc":
            pre_sr[i] = nsr; pre_bc[i] = nbc; nbc += 1

    ks = z.ks_sb

    with ExitStack() as hs:
        aux = hs.enter_context(nc.psum_tensor(f"aux{grp}", [128, 1024], F32))
        STp = [hs.enter_context(nc.psum_tensor(f"ST{grp}{i}", [128, 1024], F32))
               for i in range(2)]
        ctx = hs.enter_context(nc.psum_tensor(f"ctx{grp}", [65, 1024], F32))

        with nc.Block() as blk:
            @blk.tensor
            def _(t):
                for isched, it in enumerate(sched):
                    kind = it[0]
                    if kind == "bc":
                        j, qh, i = it[1], it[2], it[3]
                        if i == 0:
                            t.wait_ge(z.a_nc, B["a_nc"] + 6 * j + 3 * (qh + 1))
                        if pre_sr[isched] > 0:
                            t.wait_ge(z.a_sr, B["a_sr"] + pre_sr[isched])
                        if pre_bc[isched] > 0:
                            t.wait_ge(z.a_bc, B["a_bc"] + pre_bc[isched])
                        t.matmul(aux[0:64, 512:1024], z.onesf[0:1, 0:64],
                                 srow[0:1, 512 * i: 512 * (i + 1)],
                                 start=True, stop=True).then_inc(z.p_bc, 1)
                        z.c_p_bc += 1
                    elif kind == "kq":
                        j, g = it[1], it[2]
                        h = grp * 8 + j
                        if g == 0:
                            t.wait_ge(z.s_dw, z.wkq_cnt[h])
                            if j >= 1:
                                t.wait_ge(z.d_rd, B["d_rd"] + 17 * j)
                        else:
                            t.wait_ge(z.a_kq, B["a_kq"] + 12 * j + g)
                        part = 0 if g < 8 else 1
                        ts = g if g < 8 else g - 8
                        for cc in range(NC_):
                            o64 = (1 - part) * 64
                            t.matmul(aux[0:64, 0:512],
                                     wkq[h % 2][:, cc * 128 + o64: cc * 128 + o64 + 64],
                                     z.h_fm[:, cc * S + ts * 512: cc * S + (ts + 1) * 512],
                                     start=(cc == 0), stop=False)
                        boff = (1 - part) * C + h * 64
                        t.matmul(aux[0:64, 0:512], z.brow[0:1, boff:boff + 64],
                                 z.ones_bf[0:1, 0:512],
                                 start=False, stop=True).then_inc(z.p_kq, 1)
                        z.c_p_kq += 1
                    elif kind == "pa":
                        j, qt = it[1], it[2]
                        h = grp * 8 + j
                        if qt == 0:
                            t.wait_ge(z.d_rd, B["d_rd"] + 17 * j + 1)
                            t.wait_ge(z.a_kq, B["a_kq"] + 12 * (j + 1))
                        else:
                            t.wait_ge(z.d_rd, B["d_rd"] + 17 * j + qt + 1)
                        t.matmul(aux[:, 0:512],
                                 qbuf[h % 2][0:64, qt * 128:(qt + 1) * 128],
                                 ks[0:64, :],
                                 start=True, stop=True).then_inc(z.p_sa, 1)
                        z.c_p_sa += 1
                    elif kind == "sr":
                        j, qt = it[1], it[2]
                        t.wait_ge(z.d_rd, B["d_rd"] + 17 * j + qt + 2)
                        if pre_sr[isched] > 0:
                            t.wait_ge(z.a_sr, B["a_sr"] + pre_sr[isched])
                        if pre_bc[isched] > 0:
                            t.wait_ge(z.a_bc, B["a_bc"] + pre_bc[isched])
                        t.matmul(aux[0:1, 512:640],
                                 z.NM[:, qt:qt + 1], z.idf[:, :],
                                 start=True, stop=True).then_inc(z.p_sr, 1)
                        z.c_p_sr += 1
                    elif kind == "st":
                        j, qh, k = it[1], it[2], it[3]
                        h = grp * 8 + j
                        idx = 64 * j + 32 * qh + k
                        if idx >= 2:
                            t.wait_ge(z.a_ex, B["a_ex"] + idx - 1)
                        if qh == 0 and k == 0:
                            t.wait_ge(z.a_sr, B["a_sr"] + 16 * (j + 1))
                        for u in range(2):
                            mm = t.matmul(STp[k % 2][:, 512 * u:512 * (u + 1)],
                                          kbuf[h % 2][:, 128 * k:128 * (k + 1)],
                                          qbuf[h % 2][:, 1024 * qh + 512 * u: 1024 * qh + 512 * (u + 1)],
                                          start=True, stop=True)
                        mm.then_inc(z.p_st, 2)
                        z.c_p_st += 2
                    elif kind == "cx":
                        j, qh, k = it[1], it[2], it[3]
                        hh = j
                        t.wait_ge(z.a_ex, B["a_ex"] + 64 * j + 32 * qh + k + 1)
                        if k == 0:
                            w = B["d_mu"] + 2 * j + qh
                            if w > 0:
                                t.wait_ge(z.d_mu, w)
                        lhsT = vgrp[:, (k * 8 + hh) * 65:(k * 8 + hh) * 65 + 65]
                        for u in range(2):
                            mm = t.matmul(ctx[:, 512 * u:512 * (u + 1)],
                                          lhsT,
                                          ptb[k % 2][:, 512 * u: 512 * (u + 1)],
                                          start=(k == 0), stop=(k == NKC - 1))
                        mm.then_inc(z.p_ctx, 1)
                        z.c_p_ctx += 1

            @blk.scalar
            def _(s):
                for it in sched:
                    kind = it[0]
                    if kind == "bc":
                        j, qh, i = it[1], it[2], it[3]
                        bcn = B["p_bc"] + 4 * j + 2 * qh + i
                        s.wait_ge(z.p_bc, bcn + 1)
                        if i == 0 and B["d_mu"] + 2 * j + qh > 0:
                            s.wait_ge(z.d_mu, B["d_mu"] + 2 * j + qh)
                        s.copy(rec_bc[:, 512 * i:512 * (i + 1)],
                               aux[0:64, 512:1024]).then_inc(z.a_bc, 1)
                        z.c_a_bc += 1
                    elif kind == "kq":
                        j, g = it[1], it[2]
                        h = grp * 8 + j
                        s.wait_ge(z.p_kq, B["p_kq"] + 12 * j + g + 1)
                        if j >= 2:
                            s.wait_ge(z.p_st, B["p_st"] + 128 * (j - 1))
                            s.wait_ge(z.p_sa, B["p_sa"] + 16 * (j - 1))
                        part = 0 if g < 8 else 1
                        ts = g if g < 8 else g - 8
                        dst = (kbuf if part == 0 else qbuf)[h % 2][0:64, ts * 512:(ts + 1) * 512]
                        s.copy(dst, aux[0:64, 0:512]).then_inc(z.a_kq, 1)
                        z.c_a_kq += 1
                    elif kind == "sr":
                        j, qt = it[1], it[2]
                        h = grp * 8 + j
                        s.wait_ge(z.p_sr, B["p_sr"] + 16 * j + qt + 1)
                        s.copy(qbuf[h % 2][64:65, qt * 128:(qt + 1) * 128],
                               aux[0:1, 512:640]).then_inc(z.a_sr, 1)
                        z.c_a_sr += 1
                    elif kind == "st":
                        j, qh, k = it[1], it[2], it[3]
                        s.wait_ge(z.p_st, B["p_st"] + 128 * j + 2 * (32 * qh + k) + 2)
                        idx = 64 * j + 32 * qh + k
                        if idx >= 2:
                            s.wait_ge(z.p_ctx, B["p_ctx"] + idx - 1)
                        s.activation(ptb[k % 2][:, :],
                                     STp[k % 2][:, :],
                                     AF.Exp, scale=0.125).then_inc(z.a_ex, 1)
                        z.c_a_ex += 1
                    elif kind == "nc":
                        j, qh = it[1], it[2]
                        s.wait_ge(z.p_ctx, B["p_ctx"] + 64 * j + 32 * (qh + 1))
                        if B["p_bc"] + 4 * j + 2 * qh > 0:
                            s.wait_ge(z.p_bc, B["p_bc"] + 4 * j + 2 * qh)
                        sl = srow[0:1, 0:1024]
                        s.copy(sl, ctx[64:65, :]).then_inc(z.a_nc, 1)
                        z.c_a_nc += 1
                        s.wait_ge(z.a_nc, B["a_nc"] + 6 * j + 3 * qh + 1)
                        s.activation(sl, sl, AF.Ln).then_inc(z.a_nc, 1)
                        z.c_a_nc += 1
                        s.wait_ge(z.a_nc, B["a_nc"] + 6 * j + 3 * qh + 2)
                        s.activation(sl, sl, AF.Exp, scale=-1.0).then_inc(z.a_nc, 1)
                        z.c_a_nc += 1

            @blk.vector
            def _(v):
                for it in sched:
                    kind = it[0]
                    if kind == "ks":
                        j = it[1]
                        h = grp * 8 + j
                        v.wait_ge(z.a_kq, B["a_kq"] + 12 * j + 8)
                        if j >= 1:
                            v.wait_ge(z.p_sa, B["p_sa"] + 16 * j)
                        v.tensor_copy(ks[:, :],
                                      kbuf[h % 2][0:64, :].rearrange(
                                          "p (n eight) -> p n eight", eight=8)[:, :, 0:1]
                                      ).then_inc(z.d_rd, 1)
                        z.c_d_rd += 1
                    elif kind == "pa":
                        j, qt = it[1], it[2]
                        v.wait_ge(z.p_sa, B["p_sa"] + 16 * j + qt + 1)
                        if j >= 2:
                            v.wait_ge(z.p_sr, B["p_sr"] + 16 * (j - 2) + qt + 1)
                        v.tensor_reduce(z.NM[:, qt:qt + 1], aux[:, 0:512],
                                        AX.X, ALU.max, negate=True
                                        ).then_inc(z.d_rd, 1)
                        z.c_d_rd += 1
                    elif kind == "mu":
                        j, qh = it[1], it[2]
                        h = grp * 8 + j
                        v.wait_ge(z.a_bc, B["a_bc"] + 4 * j + 2 * (qh + 1))
                        if 2 * j + qh >= 4:
                            v.wait_ge(z.s_dsp, b_dsp + 16 * (j - 1))
                        elif b_dsp > 0:
                            v.wait_ge(z.s_dsp, b_dsp)
                        v.tensor_mul(z.att_t[h % 2][:, 1024 * qh:1024 * (qh + 1)],
                                     ctx[0:64, :], rec_bc[:, 0:1024]).then_inc(z.d_mu, 1)
                        z.c_d_mu += 1

            @blk.sync
            def _(sy):
                for it in sched:
                    if it[0] == "mu" and it[2] == 1:
                        j = it[1]
                        h = grp * 8 + j
                        sy.wait_ge(z.d_mu, B["d_mu"] + 2 * j + 2)
                        sy.wait_ge(z.s_dsp, z.dsp)
                        sy.dma_start(
                            out=z.attn_scr.rearrange("(t d) c -> d t c", d=64)
                                [:, :, h * 128:(h + 1) * 128],
                            in_=z.att_t[h % 2][:, :].rearrange("d (t q) -> d t q", q=128)
                            ).then_inc(z.s_dsp, 16)
                        z.dsp += 16

            @blk.gpsimd
            def _(g):
                if grp == 0:
                    g.wait_ge(z.s_dw, z.dw)
                    g.dma_start(out=wv[0].rearrange("p (cc n) -> p cc n", n=512),
                                in_=z.wqkv_h[:, 2 * C + 512:3 * C].rearrange(
                                    "(cc p) n -> p cc n", p=128)).then_inc(z.s_dw, 16)
                    z.dw += 16
                    wv_done[1] = z.dw
                for it in sched:
                    if it[0] == "pf":
                        j = it[1]
                        tgt = grp * 8 + j + 2
                        g.wait_ge(z.p_kq, B["p_kq"] + 12 * (j + 1))
                        for half_ in range(2):
                            g.wait_ge(z.s_dw, z.dw)
                            g.dma_start(out=wkq[tgt % 2].rearrange("p (cc n) -> p cc n", n=128)
                                        [:, :, half_ * 64:half_ * 64 + 64],
                                        in_=z.wqkv_h[:, half_ * C + tgt * 64: half_ * C + tgt * 64 + 64]
                                        .rearrange("(cc p) n -> p cc n", p=128)).then_inc(z.s_dw, 16)
                            z.dw += 16
                        assert z.dw == z.wkq_cnt[tgt]


def _wo(z):
    nc = z.nc
    wo_sb = z.wo_sb
    apj_t = [z.sbm(f"apj_t{i}", [128, C], BF16) for i in range(2)]
    att_in = [z.sbm(f"att_in{i}", [64, NH * 128], BF16) for i in range(2)]
    with ExitStack() as ws:
        o_ps = [ws.enter_context(nc.psum_tensor(f"o_ps{i}", [128, C], F32)) for i in range(2)]

        b_pe, b_act, b_dsp, b_drl = z.pe, z.act, z.dsp, z.drl
        with nc.Block() as blk:
            @blk.tensor
            def _(t):
                t.wait_ge(z.s_dw, z.wo_done)
                for tt in range(NTO):
                    p = tt % 2
                    if tt >= 2:
                        t.wait_ge(z.s_act, b_act + tt - 1)
                    t.wait_ge(z.s_drl, b_drl + (tt + 1) * 16)
                    for h in range(NH):
                        lhsT = att_in[p][0:64, h * 128:(h + 1) * 128]
                        rhs = wo_sb[0:64, h * C:(h + 1) * C]
                        for j in range(2):
                            mm = t.matmul(o_ps[p][:, j * 512:(j + 1) * 512], lhsT,
                                          rhs[:, j * 512:(j + 1) * 512],
                                          start=(h == 0), stop=(h == NH - 1))
                    mm.then_inc(z.s_pe, 1)
                    z.pe += 1

            @blk.scalar
            def _(s):
                for tt in range(NTO):
                    p = tt % 2
                    s.wait_ge(z.s_pe, b_pe + tt + 1)
                    if tt >= 2:
                        s.wait_ge(z.s_dsp, b_dsp + (tt - 2 + 1) * 16)
                    s.copy(apj_t[p][:, :], o_ps[p][:, :]).then_inc(z.s_act, 1)
                    z.act += 1

            @blk.sync
            def _(sy):
                sy.wait_ge(z.s_dsp, b_dsp)

                def _load(tt):
                    p = tt % 2
                    if tt >= 2:
                        sy.wait_ge(z.s_pe, b_pe + tt - 1)
                    sy.wait_ge(z.s_drl, z.drl)
                    sy.dma_start(out=att_in[p][:, :],
                                 in_=z.attn_scr[tt * 64:(tt + 1) * 64, :]
                                 ).then_inc(z.s_drl, 16)
                    z.drl += 16

                def _spill(tt):
                    p = tt % 2
                    sy.wait_ge(z.s_act, b_act + tt + 1)
                    sy.wait_ge(z.s_dsp, z.dsp)
                    sy.dma_start(out=z.apj_scr[tt * 128:(tt + 1) * 128, :],
                                 in_=apj_t[p][:, :]).then_inc(z.s_dsp, 16)
                    z.dsp += 16

                for tt in range(NTO):
                    _load(tt)
                    if tt >= 2:
                        _spill(tt - 2)
                _spill(NTO - 2)
                _spill(NTO - 1)


def _ffn(z):
    nc = z.nc
    w_in = z.sbt("w_in", [128, NC_ * 2 * HID], BF16)
    w1s = z.sbt("w1s", [128, NC_ * 2 * HID], BF16)
    w_out = z.sbt("w_out", [128, NC_ * C], BF16)
    w2b = [z.sbt(f"w2b{i}", [128, NC_ * C], BF16) for i in range(2)]
    xnsl = [z.sbt(f"xnsl{i}", [128, NC_ * SL], BF16) for i in range(2)]
    swig = z.sbt("swig", [128, NC_ * SL], BF16)
    hid = z.sbt("hid", [128, NC_ * SL], BF16)
    hide = [z.sbt(f"hide{i}", [128, NC_ * SL], BF16) for i in range(2)]
    a_full = z.sbt("a_full", [128, NC_ * SL], BF16)
    bc_all = z.sbt("bc_all", [128, E * SL], BF16)
    wdT = z.sbt("wdT", [1, E * 128], BF16)
    g_sc = z.sbt("g_sc", [128, 116], F32)
    rs_t = [z.sbt(f"f_rs{i}", [128, C], F32) for i in range(2)]
    ap_t = [z.sbt(f"f_ap{i}", [128, C], BF16) for i in range(2)]
    o_t = [z.sbt(f"f_o{i}", [128, C], F32) for i in range(2)]

    w2_done = [0, 0]
    xnsl_done = [0, 0]

    with nc.Block() as blk:
        @blk.gpsimd
        def _(g):
            g.wait_ge(z.s_dw, z.dw)
            g.dma_start(out=w_in.rearrange("p (cc n) -> p cc n", n=2 * HID),
                        in_=z.wsi_h.rearrange("(p cc) n -> p cc n", cc=NC_)).then_inc(z.s_dw, 16)
            z.dw += 16
            z.win_done = z.dw
            g.wait_ge(z.s_dw, z.dw)
            g.dma_start(out=w1s.rearrange("p (cc n) -> p cc n", n=2 * HID),
                        in_=z.w1_h.rearrange("(p cc) n -> p cc n", cc=NC_)).then_inc(z.s_dw, 16)
            z.dw += 16
            z.w1s_done = z.dw
            g.wait_ge(z.s_dw, z.dw)
            g.dma_start(out=w_out.rearrange("p (cc n) -> p cc n", n=C),
                        in_=z.wso_h.rearrange("(p cc) n -> p cc n", cc=NC_)).then_inc(z.s_dw, 16)
            z.dw += 16
            z.ffn_w_done = z.dw
            g.wait_ge(z.s_dw, z.dw)
            g.dma_start(out=w2b[0].rearrange("p (cc n) -> p cc n", n=C),
                        in_=z.w2_h[0].rearrange("(p cc) n -> p cc n", cc=NC_)).then_inc(z.s_dw, 16)
            z.dw += 16
            w2_done[0] = z.dw

        @blk.sync
        def _(sy):
            sy.wait_ge(z.s_drl, z.drl)
            sy.dma_start(out=xnsl[0].rearrange("p (cc n) -> p cc n", n=SL),
                         in_=z.xnfm_scr[0:128 * NC_, :].rearrange(
                             "(p cc) n -> p cc n", cc=NC_)).then_inc(z.s_drl, 16)
            z.drl += 16
            xnsl_done[0] = z.drl

    for sl in range(NSL):
        slb = sl % 2
        t0 = sl * SL
        with ExitStack() as fs:
          with ExitStack() as fs1:
            z_ps = [fs1.enter_context(nc.psum_tensor(f"z_ps{sl}{i}", [128, SL], F32))
                    for i in range(2)]

            for rnd, (wmat, dest) in enumerate(((w_in, swig), (w1s, hid))):
                b_pe, b_act, b_dve = z.pe, z.act, z.dve
                with nc.Block() as blk:
                    @blk.tensor
                    def _(t):
                        t.wait_ge(z.s_dw, z.win_done if rnd == 0 else z.w1s_done)
                        t.wait_ge(z.s_drl, xnsl_done[slb])
                        for j in range(16):
                            fp = j % 2
                            if j >= 2:
                                prev = j - 2
                                if prev < 8:
                                    t.wait_ge(z.s_act, b_act + prev + 1)
                                else:
                                    t.wait_ge(z.s_dve, b_dve + (prev - 8) + 1)
                            for cc in range(NC_):
                                mm = t.matmul(z_ps[fp][:, :],
                                              wmat[:, cc * 2 * HID + j * 128: cc * 2 * HID + (j + 1) * 128],
                                              xnsl[slb][:, cc * SL:(cc + 1) * SL],
                                              start=(cc == 0), stop=(cc == NC_ - 1))
                            mm.then_inc(z.s_pe, 1)
                            z.pe += 1

                    @blk.scalar
                    def _(s):
                        for j in range(8):
                            s.wait_ge(z.s_pe, b_pe + j + 1)
                            s.activation(a_full[:, j * SL:(j + 1) * SL], z_ps[j % 2][:, :],
                                         AF.Silu).then_inc(z.s_act, 1)
                            z.act += 1

                    @blk.vector
                    def _(v):
                        for j in range(8, 16):
                            v.wait_ge(z.s_pe, b_pe + j + 1)
                            v.wait_ge(z.s_act, b_act + (j - 8) + 1)
                            v.tensor_mul(dest[:, (j - 8) * SL:(j - 7) * SL],
                                         z_ps[j % 2][:, :], a_full[:, (j - 8) * SL:(j - 7) * SL]
                                         ).then_inc(z.s_dve, 1)
                            z.dve += 1

                    if rnd == 1:
                        @blk.sync
                        def _(sy):
                            if sl < NSL - 1:
                                sy.wait_ge(z.s_drl, z.drl)
                                sy.dma_start(
                                    out=xnsl[1 - slb].rearrange("p (cc n) -> p cc n", n=SL),
                                    in_=z.xnfm_scr[(sl + 1) * 128 * NC_:(sl + 2) * 128 * NC_, :]
                                    .rearrange("(p cc) n -> p cc n", cc=NC_)
                                    ).then_inc(z.s_drl, 16)
                                z.drl += 16
                                xnsl_done[1 - slb] = z.drl

            _gate(z, fs1, sl, slb, xnsl, wdT, bc_all, g_sc)

          _experts(z, fs, sl, slb, hid, hide, swig, bc_all, w2b, w_out,
                   rs_t, ap_t, o_t, t0, w2_done)


def _gate(z, fs, sl, slb, xnsl, wdT, bc_all, g_sc):
    nc = z.nc
    g_ps = fs.enter_context(nc.psum_tensor(f"g_ps{sl}", [128, 32], F32))
    wdt_ps = fs.enter_context(nc.psum_tensor(f"wdt_ps{sl}", [1, E * 128], F32))
    bc_ps = fs.enter_context(nc.psum_tensor(f"bcg_ps{sl}", [128, E * 128], F32))
    b_pe2, b_act2, b_pool, b_dve = z.pe2, z.act2, z.pool, z.dve
    GA = g_sc[:, 0:32].rearrange("p (t e) -> p t e", e=8)
    GE = g_sc[:, 32:64].rearrange("p (t e) -> p t e", e=8)
    GM = g_sc[:, 64:96].rearrange("p (t e) -> p t e", e=8)
    c_nm = g_sc[:, 96:100]
    c_sm = g_sc[:, 100:104]
    c_rc = g_sc[:, 104:108]
    c_m1 = g_sc[:, 108:112]
    c_m2 = g_sc[:, 112:116]

    with nc.Block() as blk:
        @blk.tensor
        def _(t):
            for tt in range(4):
                for cc in range(NC_):
                    mm = t.matmul(g_ps[:, tt * 8:(tt + 1) * 8],
                                  xnsl[slb][:, cc * SL + tt * 128: cc * SL + (tt + 1) * 128],
                                  z.wgate[:, cc * E:(cc + 1) * E],
                                  start=(cc == 0), stop=(cc == NC_ - 1))
            mm.then_inc(z.s_pe2, 1)
            z.pe2 += 1
            t.wait_ge(z.s_pool, b_pool + 1)
            for tt in range(4):
                if tt >= 1:
                    t.wait_ge(z.s_act2, b_act2 + 2 * tt)
                for e in range(E):
                    mm = t.matmul(wdt_ps[0:1, e * 128:(e + 1) * 128],
                                  GM[:, tt, e:e + 1], z.idf[:, :],
                                  start=True, stop=True)
                mm.then_inc(z.s_pe2, 1)
                z.pe2 += 1
                t.wait_ge(z.s_act2, b_act2 + 2 * tt + 2)
                if tt >= 1:
                    t.wait_ge(z.s_act2, b_act2 + 2 * tt + 1)
                for e in range(E):
                    mm = t.matmul(bc_ps[:, e * 128:(e + 1) * 128], z.ones_bf[0:1, 0:128],
                                  wdT[0:1, e * 128:(e + 1) * 128], start=True, stop=True)
                mm.then_inc(z.s_pe2, 1)
                z.pe2 += 1

        @blk.vector
        def _(v):
            dv = z.dve
            v.wait_ge(z.s_pe2, b_pe2 + 1)
            v.tensor_add(GA, g_ps[:, :].rearrange("p (t e) -> p t e", e=8),
                         z.gb_bc[:, :].unsqueeze(1).broadcast_to([128, 4, 8])
                         ).then_inc(z.s_dve, 1)
            dv += 1
            v.wait_ge(z.s_dve, dv)
            v.tensor_reduce(c_nm, GA, AX.X, ALU.max, negate=True).then_inc(z.s_dve, 1)
            dv += 1
            v.wait_ge(z.s_dve, dv)
            v.tensor_tensor(g_sc[:, 32:64].rearrange("p (t e) -> p t e", e=8), g_sc[:, 0:32].rearrange("p (t e) -> p t e", e=8),
                            c_nm.unsqueeze(2).broadcast_to([128, 4, 8]),
                            ALU.add).then_inc(z.s_dve, 1)
            dv += 1
            v.wait_ge(z.s_act2, b_act2 + 1)
            v.tensor_reduce(c_sm, GE, AX.X, ALU.add).then_inc(z.s_dve, 1)
            dv += 1
            v.wait_ge(z.s_dve, dv)
            v.reciprocal(c_rc, c_sm).then_inc(z.s_dve, 1)
            dv += 1
            v.wait_ge(z.s_dve, dv)
            v.tensor_reduce(c_m1, GE, AX.X, ALU.max).then_inc(z.s_dve, 1)
            dv += 1
            v.wait_ge(z.s_dve, dv)
            v.tensor_tensor(g_sc[:, 64:96].rearrange("p (t e) -> p t e", e=8), g_sc[:, 32:64].rearrange("p (t e) -> p t e", e=8),
                            c_m1.unsqueeze(2).broadcast_to([128, 4, 8]),
                            ALU.is_ge).then_inc(z.s_dve, 1)
            dv += 1
            v.wait_ge(z.s_dve, dv)
            v.scalar_tensor_tensor(g_sc[:, 64:96], g_sc[:, 64:96], -1e30,
                                   g_sc[:, 32:64], ALU.mult, ALU.add).then_inc(z.s_dve, 1)
            dv += 1
            v.wait_ge(z.s_dve, dv)
            v.tensor_reduce(c_m2, GM, AX.X, ALU.max).then_inc(z.s_dve, 1)
            dv += 1
            v.wait_ge(z.s_dve, dv)
            v.tensor_tensor(g_sc[:, 64:96].rearrange("p (t e) -> p t e", e=8), g_sc[:, 32:64].rearrange("p (t e) -> p t e", e=8),
                            c_m2.unsqueeze(2).broadcast_to([128, 4, 8]),
                            ALU.is_ge).then_inc(z.s_dve, 1)
            dv += 1
            v.wait_ge(z.s_dve, dv)
            v.tensor_mul(g_sc[:, 64:96], g_sc[:, 64:96],
                         g_sc[:, 32:64]).then_inc(z.s_dve, 1)
            dv += 1
            v.wait_ge(z.s_dve, dv)
            v.tensor_tensor(g_sc[:, 64:96].rearrange("p (t e) -> p t e", e=8), g_sc[:, 64:96].rearrange("p (t e) -> p t e", e=8),
                            c_rc.unsqueeze(2).broadcast_to([128, 4, 8]),
                            ALU.mult).then_inc(z.s_pool, 1)
            z.pool += 1
            z.dve = dv

        @blk.scalar
        def _(s):
            s.wait_ge(z.s_dve, b_dve + 3)
            s.activation(g_sc[:, 32:64], g_sc[:, 32:64], AF.Exp,
                         scale=1.0).then_inc(z.s_act2, 1)
            z.act2 += 1
            for tt in range(4):
                s.wait_ge(z.s_pe2, b_pe2 + 2 * tt + 2)
                if tt >= 1:
                    s.wait_ge(z.s_pe2, b_pe2 + 2 * tt + 1)
                s.copy(wdT[0:1, :], wdt_ps[0:1, :]).then_inc(z.s_act2, 1)
                z.act2 += 1
                s.wait_ge(z.s_pe2, b_pe2 + 2 * tt + 3)
                s.copy(bc_all.rearrange("p (e n) -> p e n", n=SL)[:, :, tt * 128:(tt + 1) * 128],
                       bc_ps.rearrange("p (e n) -> p e n", n=128)).then_inc(z.s_act2, 1)
                z.act2 += 1


def _experts(z, fs, sl, slb, hid, hide, swig, bc_all, w2b, w_out,
             rs_t, ap_t, o_t, t0, w2_done):
    nc = z.nc
    out_ps = [fs.enter_context(nc.psum_tensor(f"out_ps{sl}{i}", [128, C], F32))
              for i in range(4)]
    b_pe, b_pe2, b_dve, b_dve2 = z.pe, z.pe2, z.dve, z.dve2
    b_drl, b_dout, b_dw = z.drl, z.dout, z.dw
    w2_cnt = [w2_done[0]] + [b_dw + e * 16 for e in range(1, E)]

    with nc.Block() as blk:
        @blk.vector
        def _(v):
            for e in range(E):
                if e >= 2:
                    v.wait_ge(z.s_pe, b_pe + 1 + (e - 1) * 4)
                for cc in range(NC_):
                    tm = v.tensor_mul(hide[e % 2][:, cc * SL:(cc + 1) * SL],
                                      hid[:, cc * SL:(cc + 1) * SL],
                                      bc_all[:, e * SL:(e + 1) * SL])
                tm.then_inc(z.s_dve, 1)
                z.dve += 1
            dv2 = b_dve2
            for tt in range(4):
                p = tt % 2
                v.wait_ge(z.s_pe, b_pe + 1 + (E - 1) * 4 + tt + 1)
                v.wait_ge(z.s_drl, b_drl + (2 * tt + 2) * 16)
                if b_dout + (tt - 1) * 16 > 0:
                    v.wait_ge(z.s_dout, b_dout + (tt - 1) * 16)
                v.scalar_tensor_tensor(o_t[p][:, :], out_ps[tt][:, :], 1.0,
                                       rs_t[p][:, :], ALU.mult, ALU.add).then_inc(z.s_dve2, 1)
                dv2 += 1
                v.wait_ge(z.s_dve2, dv2)
                v.tensor_add(o_t[p][:, :], o_t[p][:, :], z.bo_bc[:, :]).then_inc(z.s_dve2, 1)
                dv2 += 1
                v.wait_ge(z.s_dve2, dv2)
                v.tensor_add(o_t[p][:, :], o_t[p][:, :], ap_t[p][:, :]).then_inc(z.s_dve2, 1)
                dv2 += 1
            z.dve2 = dv2

        @blk.tensor
        def _(t):
            t.wait_ge(z.s_dw, z.ffn_w_done)
            for tt in range(4):
                for cc in range(NC_):
                    for j in range(2):
                        mm = t.matmul(out_ps[tt][:, j * 512:(j + 1) * 512],
                                      swig[:, cc * SL + tt * 128: cc * SL + (tt + 1) * 128],
                                      w_out[:, cc * C + j * 512: cc * C + j * 512 + 512],
                                      start=(cc == 0), stop=False)
            mm.then_inc(z.s_pe, 1)
            z.pe += 1
            for e in range(E):
                t.wait_ge(z.s_dve, b_dve + e + 1)
                t.wait_ge(z.s_dw, w2_cnt[e])
                for tt in range(4):
                    for cc in range(NC_):
                        for j in range(2):
                            mm = t.matmul(out_ps[tt][:, j * 512:(j + 1) * 512],
                                          hide[e % 2][:, cc * SL + tt * 128: cc * SL + (tt + 1) * 128],
                                          w2b[e % 2][:, cc * C + j * 512: cc * C + j * 512 + 512],
                                          start=False,
                                          stop=(e == E - 1 and cc == NC_ - 1))
                    mm.then_inc(z.s_pe, 1)
                    z.pe += 1

        @blk.gpsimd
        def _(g):
            g.wait_ge(z.s_dsp, z.dsp)
            for e in range(1, E):
                g.wait_ge(z.s_pe, b_pe + 1 + max(0, e - 1) * 4)
                g.wait_ge(z.s_dw, z.dw)
                g.dma_start(out=w2b[e % 2].rearrange("p (cc n) -> p cc n", n=C),
                            in_=z.w2_h[e].rearrange("(p cc) n -> p cc n", cc=NC_)
                            ).then_inc(z.s_dw, 16)
                z.dw += 16
                assert z.dw == w2_cnt[e]
            if sl < NSL - 1:
                g.wait_ge(z.s_pe, b_pe + 1 + 7 * 4)
                g.wait_ge(z.s_dw, z.dw)
                g.dma_start(out=w2b[0].rearrange("p (cc n) -> p cc n", n=C),
                            in_=z.w2_h[0].rearrange("(p cc) n -> p cc n", cc=NC_)
                            ).then_inc(z.s_dw, 16)
                z.dw += 16
                w2_done[0] = z.dw

        @blk.sync
        def _(sy):
            sy.wait_ge(z.s_dsp, z.dsp)
            sy.wait_ge(z.s_dsp2, z.dsp2)
            for tt in range(4):
                p = tt % 2
                if b_dve2 + (tt - 2) * 3 + 1 > 0:
                    sy.wait_ge(z.s_dve2, b_dve2 + (tt - 2) * 3 + 1)
                sy.wait_ge(z.s_drl, z.drl)
                sy.dma_start(out=rs_t[p][:, :],
                             in_=z.resid_scr[t0 + tt * 128: t0 + (tt + 1) * 128, :]
                             ).then_inc(z.s_drl, 16)
                z.drl += 16
                if b_dve2 + (tt - 2) * 3 + 3 > 0:
                    sy.wait_ge(z.s_dve2, b_dve2 + (tt - 2) * 3 + 3)
                sy.wait_ge(z.s_drl, z.drl)
                sy.dma_start(out=ap_t[p][:, :],
                             in_=z.apj_scr[t0 + tt * 128: t0 + (tt + 1) * 128, :]
                             ).then_inc(z.s_drl, 16)
                z.drl += 16
            for tt in range(4):
                p = tt % 2
                sy.wait_ge(z.s_dve2, b_dve2 + tt * 3 + 3)
                sy.wait_ge(z.s_dout, z.dout)
                sy.dma_start(out=z.out_h[t0 + tt * 128: t0 + (tt + 1) * 128, :],
                             in_=o_t[p][:, :]).then_inc(z.s_dout, 16)
                z.dout += 16


def _early_out(z):
    nc = z.nc
    buf = z.sbt("early", [128, C], F32)
    with nc.Block() as blk:
        @blk.sync
        def _(sy):
            for t in range(NTO):
                sy.wait_ge(z.s_drl, z.drl)
                sy.dma_start(out=buf[:, :],
                             in_=z.resid_scr[t * 128:(t + 1) * 128, :]).then_inc(z.s_drl, 16)
                z.drl += 16
                sy.wait_ge(z.s_drl, z.drl)
                sy.wait_ge(z.s_dout, z.dout)
                sy.dma_start(out=z.out_h[t * 128:(t + 1) * 128, :],
                             in_=buf[:, :]).then_inc(z.s_dout, 16)
                z.dout += 16
                sy.wait_ge(z.s_dout, z.dout)


def kernel(L0_attn_x, L0_attn_gn_gamma, L0_attn_gn_beta, L0_attn_W_qkv,
           L0_attn_b_qkv, L0_attn_W_o, L0_attn_b_o, L0_ffn_w_shared_in,
           L0_ffn_w_shared_out, L0_ffn_w1_shared, L0_ffn_w2, L0_ffn_w_gate,
           L0_ffn_gate_bias):
    bf = ml_dtypes.bfloat16
    x = np.ascontiguousarray(np.asarray(L0_attn_x, dtype=np.float32))
    nc = build_program()

    def permrows(w):
        cc = w.shape[0] // 128
        return np.ascontiguousarray(
            w.reshape(cc, 128, w.shape[-1]).transpose(1, 0, 2).reshape(w.shape))

    common = {
        "wqkv": np.ascontiguousarray(L0_attn_W_qkv, dtype=np.float32),
        "bqkv": np.ascontiguousarray(L0_attn_b_qkv, dtype=np.float32).reshape(1, 3 * C),
        "wo": np.ascontiguousarray(L0_attn_W_o, dtype=np.float32),
        "bo": np.ascontiguousarray(L0_attn_b_o, dtype=np.float32).reshape(1, C),
        "gamma_cols": np.ascontiguousarray(
            np.asarray(L0_attn_gn_gamma, dtype=np.float32).reshape(NC_, 128).T),
        "beta_cols": np.ascontiguousarray(
            np.asarray(L0_attn_gn_beta, dtype=np.float32).reshape(NC_, 128).T),
        "w_sh_in": permrows(np.asarray(L0_ffn_w_shared_in, dtype=np.float32)),
        "w_sh_out": permrows(np.asarray(L0_ffn_w_shared_out, dtype=np.float32)),
        "w1_sh": permrows(np.asarray(L0_ffn_w1_shared, dtype=np.float32)),
        "w2": np.stack([permrows(np.asarray(L0_ffn_w2[e], dtype=np.float32))
                        for e in range(E)]),
        "w_gate": np.ascontiguousarray(L0_ffn_w_gate, dtype=np.float32),
        "gate_bias": np.ascontiguousarray(L0_ffn_gate_bias, dtype=np.float32).reshape(1, E),
        "identf": np.eye(128, dtype=np.float32),
        "identb": np.eye(128, dtype=bf),
        "ind4": np.ascontiguousarray(np.repeat(np.eye(4, dtype=np.float32), 32, axis=0)),
        "ind4t": np.ascontiguousarray(np.repeat(np.eye(4, dtype=np.float32), 32, axis=0).T),
        "onesf": np.ones((1, 128), dtype=np.float32),
    }
    in_maps = []
    for c in range(8):
        b, m = c // 2, c % 2
        xb = x[b]
        xp = np.concatenate([xb[m * HALF:(m + 1) * HALF], xb[(1 - m) * HALF:(2 - m) * HALF]])
        d = dict(common)
        d["x"] = np.ascontiguousarray(xp)
        in_maps.append(d)

    res = run_bass_kernel_spmd(nc, in_maps, core_ids=list(range(8)))
    out = np.empty((B, S, C), dtype=np.float32)
    for c in range(8):
        b, m = c // 2, c % 2
        out[b, m * HALF:(m + 1) * HALF, :] = res.results[c]["out"]
    return out


# version 33
# speedup vs baseline: 1.0705x; 1.0705x over previous
import numpy as np
import ml_dtypes
from contextlib import ExitStack

import concourse.bass as bass
import concourse.mybir as mybir
from concourse.bass_utils import run_bass_kernel_spmd

F32 = mybir.dt.float32
BF16 = mybir.dt.bfloat16
AF = mybir.ActivationFunctionType
ALU = mybir.AluOpType
AX = mybir.AxisListType

B, S, C = 4, 4096, 1024
NH, DH, NG, GW = 16, 64, 32, 32
HALF = S // 2
NT = S // 128
NTO = HALF // 128
NC_ = C // 128
E, HID = 8, 1024
NKC = S // 128
SL = 512
NSL = HALF // SL
LN_EPS, GN_EPS = 1e-5, 1e-6
STAGE = 4


class Ctx:
    pass


def build_program(stage=STAGE):
    nc = bass.Bass("TRN2", target_bir_lowering=False)
    z = Ctx()
    z.nc = nc
    z.stage = stage

    z.x_h = nc.dram_tensor("x", [S, C], F32, kind="ExternalInput").ap()
    z.wqkv_h = nc.dram_tensor("wqkv", [C, 3 * C], F32, kind="ExternalInput").ap()
    z.bqc_h = nc.dram_tensor("bq_cols", [64, 2 * NH], F32, kind="ExternalInput").ap()
    z.bqkv_h = nc.dram_tensor("bqkv", [1, 3 * C], F32, kind="ExternalInput").ap()
    z.wo_h = nc.dram_tensor("wo", [C, C], F32, kind="ExternalInput").ap()
    z.bo_h = nc.dram_tensor("bo", [1, C], F32, kind="ExternalInput").ap()
    z.gam_h = nc.dram_tensor("gamma_cols", [128, NC_], F32, kind="ExternalInput").ap()
    z.bet_h = nc.dram_tensor("beta_cols", [128, NC_], F32, kind="ExternalInput").ap()
    z.wsi_h = nc.dram_tensor("w_sh_in", [C, 2 * HID], F32, kind="ExternalInput").ap()
    z.wso_h = nc.dram_tensor("w_sh_out", [HID, C], F32, kind="ExternalInput").ap()
    z.w1_h = nc.dram_tensor("w1_sh", [C, 2 * HID], F32, kind="ExternalInput").ap()
    z.w2_h = nc.dram_tensor("w2", [E, HID, C], F32, kind="ExternalInput").ap()
    z.wg_h = nc.dram_tensor("w_gate", [C, E], F32, kind="ExternalInput").ap()
    z.gb_h = nc.dram_tensor("gate_bias", [1, E], F32, kind="ExternalInput").ap()
    z.idf_h = nc.dram_tensor("identf", [128, 128], F32, kind="ExternalInput").ap()
    z.idb_h = nc.dram_tensor("identb", [128, 128], BF16, kind="ExternalInput").ap()
    z.ind4_h = nc.dram_tensor("ind4", [128, 4], F32, kind="ExternalInput").ap()
    z.ind4t_h = nc.dram_tensor("ind4t", [4, 128], F32, kind="ExternalInput").ap()
    z.onesf_h = nc.dram_tensor("onesf", [1, 128], F32, kind="ExternalInput").ap()
    z.out_h = nc.dram_tensor("out", [HALF, C], F32, kind="ExternalOutput").ap()
    z.resid_scr = nc.dram_tensor("resid_scr", [HALF, C], F32, kind="Internal").ap()
    z.attn_scr = nc.dram_tensor("attn_scr", [NTO * 64, HALF], BF16, kind="Internal").ap()
    z.xnfm_scr = nc.dram_tensor("xnfm_scr", [NSL * 128 * NC_, SL], BF16,
                                kind="Internal").ap()
    z.apj_scr = nc.dram_tensor("apj_scr", [HALF, C], BF16, kind="Internal").ap()

    with ExitStack() as top:
        z.top = top
        sbt = lambda name, shape, dt: top.enter_context(nc.sbuf_tensor(name, shape, dt))
        z.sbt = sbt
        z.sbt2 = lambda name, shape, dt: top.enter_context(
            nc.sbuf_tensor(name, shape, dt, side="right"))
        z.mid = ExitStack()
        z.sbm = lambda name, shape, dt: z.mid.enter_context(nc.sbuf_tensor(name, shape, dt))

        z.idf = sbt("idf_s", [128, 128], F32)
        z.idb = sbt("idb_s", [128, 128], BF16)
        z.ind4 = sbt("ind4_s", [128, 4], F32)
        z.ind4t = sbt("ind4t_s", [4, 128], F32)
        z.onesf = sbt("onesf_s", [1, 128], F32)
        z.ones_bf = sbt("ones_bf", [128, 512], BF16)
        z.eps5 = sbt("eps5", [128, 1], F32)
        z.eps6 = sbt("eps6", [4, 1], F32)
        z.bq_cols = sbt("bq_cols_s", [64, 2 * NH], F32)
        z.gam = sbt("gam", [128, NC_], F32)
        z.bet = sbt("bet", [128, NC_], F32)
        z.bo_bc = sbt("bo_bc", [128, C], F32)
        z.gb_bc = sbt("gb_bc", [128, E], F32)
        z.wgate = sbt("wgate", [128, NC_ * E], BF16)
        z.NM = sbt("NM", [128, NTO], F32)
        z.pre = sbt("pre", [128, NC_], F32)
        z.off = sbt("off", [128, NC_], F32)
        z.mcols = sbt("mcols", [128, NC_], F32)
        z.sm_nm = sbt("sm_nm", [128, 2], F32)
        z.gst = sbt("gst", [128, NC_], F32)
        z.g4 = sbt("g4", [4, 3 * NC_], F32)

        def sem(name):
            h = top.enter_context(nc.semaphore(name))
            return h
        z.brow = z.sbm("brow", [1, 3 * C], BF16)
        z.s_dx, z.dx = sem("s_dx"), 0
        z.s_dw, z.dw = sem("s_dw"), 0
        z.s_dsp, z.dsp = sem("s_dsp"), 0
        z.s_dsp2, z.dsp2 = sem("s_dsp2"), 0
        z.s_drl, z.drl = sem("s_drl"), 0
        z.s_dout, z.dout = sem("s_dout"), 0
        z.s_pe, z.pe = sem("s_pe"), 0
        z.s_pe2, z.pe2 = sem("s_pe2"), 0
        z.s_act, z.act = sem("s_act"), 0
        z.s_act2, z.act2 = sem("s_act2"), 0
        z.s_dve, z.dve = sem("s_dve"), 0
        z.s_dve2, z.dve2 = sem("s_dve2"), 0
        z.s_pool, z.pool = sem("s_pool"), 0

        _c0(z)
        _p0(z)
        if stage >= 1:
            _p1(z)
            _p2a(z)
        z.xnfm_scope.close()
        if stage >= 2:
            _attention(z)
            z.att_scope.close()
        if hasattr(z, 'hfm_scope'):
            z.hfm_scope.close()
        if stage >= 3:
            _wo(z)
        z.mid.close()
        if stage >= 4:
            _ffn(z)
        else:
            _early_out(z)

        with nc.Block() as blk:
            @blk.sync
            def _(sy):
                sy.wait_ge(z.s_dout, z.dout)
    return nc


def _c0(z):
    nc = z.nc
    with nc.Block() as blk:
        @blk.sync
        def _(sy):
            for dst, src in [(z.idf, z.idf_h), (z.idb, z.idb_h), (z.ind4, z.ind4_h),
                             (z.ind4t, z.ind4t_h), (z.onesf, z.onesf_h),
                             (z.gam, z.gam_h), (z.bet, z.bet_h), (z.bq_cols, z.bqc_h)]:
                sy.wait_ge(z.s_dx, z.dx)
                sy.dma_start(out=dst[:, :], in_=src).then_inc(z.s_dx, 16)
                z.dx += 16

        @blk.gpsimd
        def _(g):
            g.memset(z.ones_bf[:, :], 1.0)
            g.memset(z.eps5[:, :], LN_EPS)
            g.memset(z.eps6[:, :], GN_EPS)
            g.wait_ge(z.s_dw, z.dw)
            g.dma_start(out=z.brow[:, :], in_=z.bqkv_h).then_inc(z.s_dw, 16)
            z.dw += 16
            g.wait_ge(z.s_dw, z.dw)
            g.dma_start(out=z.wgate.rearrange("p (cc e) -> p cc e", e=E),
                        in_=z.wg_h.rearrange("(cc p) e -> p cc e", p=128)).then_inc(z.s_dw, 16)
            z.dw += 16


def _p0(z):
    nc = z.nc
    z.xnfm_scope = ExitStack()
    z.xn_fm = z.xnfm_scope.enter_context(
        z.nc.sbuf_tensor("xn_fm", [128, NC_ * S], BF16))

    with ExitStack() as ps:
        z.xn_tm = ps.enter_context(nc.sbuf_tensor("xn_tm", [128, NT * C], BF16, side="right"))
        z.bo_f32 = ps.enter_context(nc.sbuf_tensor("bo_f32", [1, C], F32))
        z.gb_f32 = ps.enter_context(nc.sbuf_tensor("gb_f32", [1, E], F32))
        x_t = [ps.enter_context(nc.sbuf_tensor(f"x_t{i}", [128, 4 * C], F32)) for i in range(2)]
        rs_t = [ps.enter_context(nc.sbuf_tensor(f"rs_t{i}", [128, 2 * C], F32)) for i in range(2)]
        nmean = ps.enter_context(nc.sbuf_tensor("nmean", [128, 4], F32))
        sumsq = ps.enter_context(nc.sbuf_tensor("sumsq", [128, 4], F32))
        rstd = ps.enter_context(nc.sbuf_tensor("rstd", [128, 4], F32))
        m2c = ps.enter_context(nc.sbuf_tensor("m2c", [128, 4], F32))
        varc = ps.enter_context(nc.sbuf_tensor("varc", [128, 4], F32))
        sq_ps = ps.enter_context(nc.psum_tensor("sq_ps", [128, C], F32))
        tr_ps = [ps.enter_context(nc.psum_tensor(f"tr_ps{i}", [128, C], F32)) for i in range(3)]

        b_dx, b_dve, b_dve2, b_act, b_dsp2 = z.dx, z.dve, z.dve2, z.act, z.dsp2
        b_pe, b_act2, b_pool = z.pe, z.act2, z.pool
        c_red, c_var, c_rec = {}, {}, {}
        c_sq, c_mul, c_m2, c_sqrt = {}, {}, {}, {}
        c_ts, c_add = {}, {}
        dv = b_dve
        for t in range(NT):
            dv += 1; c_red[t] = dv
            if t >= 1:
                dv += 1; c_var[t - 1] = dv
                dv += 1; c_rec[t - 1] = dv
        dv += 1; c_var[NT - 1] = dv
        dv += 1; c_rec[NT - 1] = dv
        da = b_act
        for t in range(NT):
            da += 1; c_sq[t] = da
            da += 1; c_mul[t] = da
            da += 1; c_m2[t] = da
            if t >= 1:
                da += 1; c_sqrt[t - 1] = da
        da += 1; c_sqrt[NT - 1] = da
        dp = b_pool
        for t in range(NT):
            dp += 1; c_ts[t] = dp
            if 1 <= t <= NTO:
                dp += 1; c_add[t - 1] = dp

        with nc.Block() as blk:
            @blk.sync
            def _(sy):
                for dst, srch in [(z.bo_f32, z.bo_h), (z.gb_f32, z.gb_h)]:
                    sy.wait_ge(z.s_dx, z.dx)
                    sy.dma_start(out=dst[:, :], in_=srch).then_inc(z.s_dx, 16)
                    z.dx += 16
                for g in range(NT // 4):
                    p = g % 2
                    if g >= 2:
                        L = 4 * g - 5
                        sy.wait_ge(z.s_dve, c_red[L])
                        sy.wait_ge(z.s_act, c_sq[L])
                        sy.wait_ge(z.s_pool, c_ts[L] if L >= NTO else c_add[L])
                        if L >= NTO:
                            sy.wait_ge(z.s_dve2, b_dve2 + L + 1)
                    sy.wait_ge(z.s_dx, z.dx)
                    sy.dma_start(out=x_t[p][:, :].rearrange("p (two c) -> p two c", c=C),
                                 in_=z.x_h[g * 512:(g + 1) * 512, :].rearrange(
                                     "(two p) c -> p two c", p=128)).then_inc(z.s_dx, 16)
                    z.dx += 16

            @blk.gpsimd
            def _(gq):
                def xs_rr(t):
                    g, sb = t // 4, t % 4
                    xs = x_t[g % 2][:, sb * C:(sb + 1) * C]
                    g2, sb2 = t // 2, t % 2
                    rr = rs_t[g2 % 2][:, sb2 * C:(sb2 + 1) * C] if t < NTO else xs
                    return xs, rr, g2, sb2, sb
                for t in range(NT):
                    xs, rr, g2, sb2, sb = xs_rr(t)
                    gq.wait_ge(z.s_dve, c_rec[t])
                    if t < NTO and g2 >= 2 and sb2 == 0:
                        gq.wait_ge(z.s_dsp2, b_dsp2 + (g2 - 1) * 16)
                    gq.tensor_scalar(rr, xs, nmean[:, sb:sb + 1],
                                     rstd[:, sb:sb + 1], ALU.add, ALU.mult
                                     ).then_inc(z.s_pool, 1)
                    if 1 <= t <= NTO:
                        ta = t - 1
                        xs2, rr2, g2b, sb2b, _ = xs_rr(ta)
                        gq.wait_ge(z.s_dve2, b_dve2 + ta + 1)
                        gq.wait_ge(z.s_pool, c_ts[ta])
                        gq.tensor_add(rr2, rr2, xs2).then_inc(z.s_pool, 1)
                        if t % 2 == 0:
                            gsp = (t - 2) // 2
                            gq.wait_ge(z.s_pool, c_add[t - 1])
                            gq.wait_ge(z.s_dsp2, z.dsp2)
                            gq.dma_start(out=z.resid_scr[gsp * 256:(gsp + 1) * 256, :].rearrange(
                                             "(two p) c -> p two c", p=128),
                                         in_=rs_t[gsp % 2][:, :].rearrange(
                                             "p (two c) -> p two c", c=C)
                                         ).then_inc(z.s_dsp2, 16)
                            z.dsp2 += 16
                z.pool = dp

            @blk.vector
            def _(v):
                def emit_var(t):
                    sb = t % 4
                    v.wait_ge(z.s_act, c_m2[t])
                    v.tensor_scalar(varc[:, sb:sb + 1], sumsq[:, sb:sb + 1], 1.0 / C,
                                    m2c[:, sb:sb + 1], ALU.mult, ALU.subtract
                                    ).then_inc(z.s_dve, 1)

                def emit_rec(t):
                    sb = t % 4
                    v.wait_ge(z.s_act, c_sqrt[t])
                    v.reciprocal(rstd[:, sb:sb + 1], rstd[:, sb:sb + 1]).then_inc(z.s_dve, 1)

                def emit_copy(t):
                    g, sb = t // 4, t % 4
                    xs = x_t[g % 2][:, sb * C:(sb + 1) * C]
                    g2, sb2 = t // 2, t % 2
                    rr = rs_t[g2 % 2][:, sb2 * C:(sb2 + 1) * C] if t < NTO else xs
                    v.wait_ge(z.s_pool, c_ts[t])
                    v.tensor_copy(z.xn_tm[:, t * C:(t + 1) * C], rr).then_inc(z.s_dve2, 1)

                for t in range(NT):
                    g, sb = t // 4, t % 4
                    xs = x_t[g % 2][:, sb * C:(sb + 1) * C]
                    v.wait_ge(z.s_dx, b_dx + 32 + (g + 1) * 16)
                    v.tensor_reduce(nmean[:, sb:sb + 1], xs, AX.X, ALU.add,
                                    negate=True).then_inc(z.s_dve, 1)
                    if t >= 1:
                        emit_var(t - 1)
                        emit_rec(t - 1)
                    if t >= 2:
                        emit_copy(t - 2)
                emit_var(NT - 1)
                emit_rec(NT - 1)
                emit_copy(NT - 2)
                emit_copy(NT - 1)
                z.dve = dv
                z.dve2 = b_dve2 + NT

            @blk.scalar
            def _(s):
                def emit_evac(tt):
                    s.wait_ge(z.s_pe, b_pe + tt + 1)
                    s.copy(z.xn_fm.rearrange("p (fc ss) -> p fc ss", ss=S)
                           [:, :, tt * 128:(tt + 1) * 128],
                           tr_ps[tt % 3][:, :].rearrange("p (fc n) -> p fc n", n=128)
                           ).then_inc(z.s_act2, 1)

                for t in range(NT):
                    g, sb = t // 4, t % 4
                    xs = x_t[g % 2][:, sb * C:(sb + 1) * C]
                    s.wait_ge(z.s_dx, b_dx + 32 + (g + 1) * 16)
                    s.activation(sq_ps[:, :], xs, AF.Square, scale=1.0,
                                 accum_out=sumsq[:, sb:sb + 1]).then_inc(z.s_act, 1)
                    s.wait_ge(z.s_dve, c_red[t])
                    s.mul(nmean[:, sb:sb + 1], nmean[:, sb:sb + 1], 1.0 / C).then_inc(z.s_act, 1)
                    s.wait_ge(z.s_act, c_mul[t])
                    s.activation(m2c[:, sb:sb + 1], nmean[:, sb:sb + 1], AF.Square,
                                 scale=1.0).then_inc(z.s_act, 1)
                    if t >= 1:
                        sbp = (t - 1) % 4
                        s.wait_ge(z.s_dve, c_var[t - 1])
                        s.activation(rstd[:, sbp:sbp + 1], varc[:, sbp:sbp + 1], AF.Sqrt,
                                     bias=z.eps5[:, :], scale=1.0).then_inc(z.s_act, 1)
                    if t >= 2:
                        emit_evac(t - 2)
                sbp = (NT - 1) % 4
                s.wait_ge(z.s_dve, c_var[NT - 1])
                s.activation(rstd[:, sbp:sbp + 1], varc[:, sbp:sbp + 1], AF.Sqrt,
                             bias=z.eps5[:, :], scale=1.0).then_inc(z.s_act, 1)
                emit_evac(NT - 2)
                emit_evac(NT - 1)
                z.act = da
                z.act2 = b_act2 + NT

            @blk.tensor
            def _(t):
                t.wait_ge(z.s_dx, b_dx + 16)
                for tt in range(NT):
                    t.wait_ge(z.s_dve2, b_dve2 + tt + 1)
                    if tt >= 3:
                        t.wait_ge(z.s_act2, b_act2 + tt - 2)
                    for fc in range(NC_):
                        mm = t.matmul(tr_ps[tt % 3][:, fc * 128:(fc + 1) * 128],
                                      z.xn_tm[:, tt * C + fc * 128: tt * C + fc * 128 + 128],
                                      z.idb[:, :], start=True, stop=True)
                    mm.then_inc(z.s_pe, 1)
                    z.pe += 1
                t.wait_ge(z.s_act2, b_act2 + NT)
                for jj in range(2):
                    t.matmul(tr_ps[0][:, jj * 512:(jj + 1) * 512], z.onesf[:, :],
                             z.bo_f32[0:1, jj * 512:(jj + 1) * 512], start=True, stop=True)
                t.matmul(tr_ps[1][:, 0:8], z.onesf[:, :], z.gb_f32[:, :],
                         start=True, stop=True).then_inc(z.s_pe, 1)
                z.pe += 1

            @blk.vector
            def _(v2):
                v2.wait_ge(z.s_pe, b_pe + NT + 1)
                v2.tensor_copy(z.bo_bc[:, :], tr_ps[0][:, :])
                v2.tensor_copy(z.gb_bc[:, :], tr_ps[1][:, 0:8])


def _p1(z):
    nc = z.nc
    with ExitStack() as ps:
        g_ps = ps.enter_context(nc.psum_tensor("g_ps", [4, 2 * NC_], F32))
        e_ps = ps.enter_context(nc.psum_tensor("e_ps", [128, 2 * NC_], F32))
        sqo = ps.enter_context(nc.sbuf_tensor("sqo", [128, S], BF16))
        gst2 = ps.enter_context(nc.sbuf_tensor("gst2", [128, NC_], F32))
        m2g = ps.enter_context(nc.sbuf_tensor("m2g", [4, NC_], F32))
        varg = ps.enter_context(nc.sbuf_tensor("varg", [4, NC_], F32))

        b_dve, b_act, b_pe = z.dve, z.act, z.pe
        kinv = 1.0 / (S * GW)
        with nc.Block() as blk:
            @blk.vector
            def _(v):
                dv = b_dve
                for fc in range(NC_):
                    v.tensor_reduce(z.gst[:, fc:fc + 1], z.xn_fm[:, fc * S:(fc + 1) * S],
                                    AX.X, ALU.add).then_inc(z.s_dve, 1)
                    dv += 1
                v.wait_ge(z.s_pe, b_pe + 2)
                v.wait_ge(z.s_act, b_act + 10)
                v.scalar_tensor_tensor(varg[:, :], g_ps[0:4, NC_:2 * NC_], kinv,
                                       m2g[:, :], ALU.mult, ALU.subtract
                                       ).then_inc(z.s_dve, 1)
                dv += 1
                v.wait_ge(z.s_act, b_act + 11)
                v.reciprocal(z.g4[:, 2 * NC_:3 * NC_],
                             z.g4[:, NC_:2 * NC_]).then_inc(z.s_dve, 1)
                dv += 1
                v.wait_ge(z.s_pe, b_pe + 4)
                v.tensor_mul(z.pre[:, :], e_ps[:, NC_:2 * NC_],
                             z.gam[:, :]).then_inc(z.s_dve, 1)
                dv += 1
                v.wait_ge(z.s_dve, dv)
                v.wait_ge(z.s_act, b_act + 12)
                v.tensor_mul(z.off[:, :], z.mcols[:, :], z.pre[:, :]).then_inc(z.s_dve, 1)
                dv += 1
                v.wait_ge(z.s_dve, dv)
                v.tensor_add(z.off[:, :], z.off[:, :], z.bet[:, :]).then_inc(z.s_dve, 1)
                dv += 1
                z.dve = dv

            @blk.scalar
            def _(s):
                da = b_act
                s.wait_ge(z.s_dsp2, z.dsp2)
                for fc in range(NC_):
                    if fc > 0:
                        s.wait_ge(z.s_act, da)
                    s.activation(sqo[:, :], z.xn_fm[:, fc * S:(fc + 1) * S], AF.Square,
                                 scale=1.0, accum_out=gst2[:, fc:fc + 1]).then_inc(z.s_act, 1)
                    da += 1
                s.wait_ge(z.s_pe, b_pe + 2)
                s.mul(z.g4[:, 0:NC_], g_ps[0:4, 0:NC_], -kinv).then_inc(z.s_act, 1)
                da += 1
                s.wait_ge(z.s_act, da)
                s.activation(m2g[:, :], z.g4[0:4, 0:NC_], AF.Square,
                             scale=1.0).then_inc(z.s_act, 1)
                da += 1
                s.wait_ge(z.s_dve, b_dve + NC_ + 1)
                s.activation(z.g4[:, NC_:2 * NC_], varg[0:4, :], AF.Sqrt,
                             bias=z.eps6[:, :], scale=1.0).then_inc(z.s_act, 1)
                da += 1
                s.wait_ge(z.s_pe, b_pe + 4)
                s.copy(z.mcols[:, :], e_ps[:, 0:NC_]).then_inc(z.s_act, 1)
                da += 1
                z.act = da

            @blk.tensor
            def _(t):
                t.wait_ge(z.s_dve, b_dve + NC_)
                t.matmul(g_ps[:, 0:NC_], z.ind4[:, :], z.gst[:, :],
                         start=True, stop=True).then_inc(z.s_pe, 1)
                z.pe += 1
                t.wait_ge(z.s_act, b_act + NC_)
                t.matmul(g_ps[:, NC_:2 * NC_], z.ind4[:, :], gst2[:, :],
                         start=True, stop=True).then_inc(z.s_pe, 1)
                z.pe += 1
                t.wait_ge(z.s_act, b_act + NC_ + 1)
                t.matmul(e_ps[:, 0:NC_], z.ind4t[:, :], z.g4[0:4, 0:NC_],
                         start=True, stop=True).then_inc(z.s_pe, 1)
                z.pe += 1
                t.wait_ge(z.s_dve, b_dve + NC_ + 2)
                t.matmul(e_ps[:, NC_:2 * NC_], z.ind4t[:, :], z.g4[0:4, 2 * NC_:3 * NC_],
                         start=True, stop=True).then_inc(z.s_pe, 1)
                z.pe += 1


def _p2a(z):
    nc = z.nc
    z.hfm_scope = ExitStack()
    h_fm = z.hfm_scope.enter_context(
        nc.sbuf_tensor("h_fm", [128, NC_ * S], BF16, side="right"))
    b_dve = z.dve
    with nc.Block() as blk:
        @blk.sync
        def _(sy):
            for sq in range(NSL):
                sy.wait_ge(z.s_dsp, z.dsp)
                sy.dma_start(out=z.xnfm_scr[sq * 128 * NC_:(sq + 1) * 128 * NC_, :]
                             .rearrange("(p cc) n -> p cc n", cc=NC_),
                             in_=z.xn_fm.rearrange("p (cc s) -> p cc s", s=S)
                             [:, :, sq * SL:(sq + 1) * SL]).then_inc(z.s_dsp, 16)
                z.dsp += 16
            z.p2a_dsp = z.dsp

        @blk.vector
        def _(v):
            for fc in range(5):
                v.tensor_scalar(h_fm[:, fc * S:(fc + 1) * S],
                                z.xn_fm[:, fc * S:(fc + 1) * S],
                                z.pre[:, fc:fc + 1], z.off[:, fc:fc + 1],
                                ALU.mult, ALU.add).then_inc(z.s_dve, 1)
                z.dve += 1

        @blk.gpsimd
        def _(g):
            for fc in range(5, NC_):
                g.tensor_scalar(h_fm[:, fc * S:(fc + 1) * S],
                                z.xn_fm[:, fc * S:(fc + 1) * S],
                                z.pre[:, fc:fc + 1], z.off[:, fc:fc + 1],
                                ALU.mult, ALU.add).then_inc(z.s_dve, 1)
                z.dve += 1
    z.apply_done = z.dve
    z.h_fm = h_fm


def _attention(z):
    nc = z.nc
    z.wo_sb = z.sbm("wo_sb", [64, NH * C], BF16)
    z.att_scope = ExitStack()
    sba = lambda name, shape, dt: z.att_scope.enter_context(nc.sbuf_tensor(name, shape, dt))
    z.att_t = [sba(f"att_t{i}", [64, HALF], BF16) for i in range(2)]
    vgrp = sba("vgrp", [128, NT * 8 * 65], BF16)
    kbuf = [sba(f"kp{i}", [65, S], BF16) for i in range(2)]
    qbuf = [sba(f"qp{i}", [65, HALF], BF16) for i in range(2)]
    ptb = [sba(f"pt{i}", [128, 1024], BF16) for i in range(2)]
    wkq = [sba(f"wkq{i}", [128, NC_ * 128], BF16) for i in range(2)]
    wv = [sba("wv0", [128, NC_ * 512], BF16)]
    wv.append(wv[0])
    srow = sba("srow", [1, 1024], F32)
    rec_bc = sba("rec_bc", [64, 1024], F32)
    z.ks_sb = sba("ks_sb", [64, 512], BF16)
    z.smn = sba("smn", [128, 8], F32)

    sem = lambda name: z.att_scope.enter_context(nc.semaphore(name))
    for nm in ("p_kq", "p_sa", "p_sr", "p_st", "p_ctx", "p_bc",
               "a_kq", "a_sr", "a_ex", "a_nc", "a_bc",
               "d_rd", "d_nm", "d_mu"):
        setattr(z, nm, sem(nm))
        setattr(z, "c_" + nm, 0)

    wv_done = [0, 0]
    z.wkq_cnt = {}

    with nc.Block() as blk:
        @blk.gpsimd
        def _(g):
            g.wait_ge(z.s_dsp2, z.dsp2)
            g.wait_ge(z.s_dsp, z.p2a_dsp)
            g.wait_ge(z.s_dve, z.apply_done)
            g.memset(vgrp.rearrange("p (t h o) -> p t h o", h=8, o=65)[:, :, :, 64:65], 1.0)
            g.memset(kbuf[0][64:65, :], 1.0)
            g.memset(kbuf[1][64:65, :], 1.0)
            g.wait_ge(z.s_dw, z.dw)
            g.dma_start(out=wv[0].rearrange("p (cc n) -> p cc n", n=512),
                        in_=z.wqkv_h[:, 2 * C:2 * C + 512].rearrange("(cc p) n -> p cc n", p=128)
                        ).then_inc(z.s_dw, 16)
            z.dw += 16
            wv_done[0] = z.dw
            for hh0 in range(2):
                for half_ in range(2):
                    g.wait_ge(z.s_dw, z.dw)
                    g.dma_start(out=wkq[hh0].rearrange("p (cc n) -> p cc n", n=128)
                                [:, :, half_ * 64:half_ * 64 + 64],
                                in_=z.wqkv_h[:, half_ * C + hh0 * 64: half_ * C + hh0 * 64 + 64]
                                .rearrange("(cc p) n -> p cc n", p=128)).then_inc(z.s_dw, 16)
                    z.dw += 16
                z.wkq_cnt[hh0] = z.dw

    for grp in range(2):
        with ExitStack() as vs:
            v_ps = [vs.enter_context(nc.psum_tensor(f"v_ps{grp}{i}", [128, 512], F32))
                    for i in range(2)]
            b_pe, b_act = z.pe, z.act
            with nc.Block() as blk:
                @blk.tensor
                def _(t):
                    t.wait_ge(z.s_dw, wv_done[grp])
                    t.wait_ge(z.s_dw, 16)
                    for tt in range(NT):
                        p = tt % 2
                        if tt >= 2:
                            t.wait_ge(z.s_act, b_act + tt - 1)
                        for cc in range(NC_):
                            t.matmul(v_ps[p][:, :],
                                     z.h_fm[:, cc * S + tt * 128: cc * S + (tt + 1) * 128],
                                     wv[grp][:, cc * 512:(cc + 1) * 512],
                                     start=(cc == 0), stop=False)
                        t.matmul(v_ps[p][:, :], z.ones_bf[0:1, 0:128],
                                 z.brow[0:1, 2 * C + grp * 512: 2 * C + grp * 512 + 512],
                                 start=False, stop=True).then_inc(z.s_pe, 1)
                        z.pe += 1

                @blk.scalar
                def _(s):
                    for tt in range(NT):
                        p = tt % 2
                        s.wait_ge(z.s_pe, b_pe + tt + 1)
                        dst = vgrp[:, tt * 520: tt * 520 + 520].rearrange(
                            "p (h o) -> p h o", o=65)[:, :, 0:64]
                        s.copy(dst, v_ps[p][:, :].rearrange("p (h d) -> p h d", d=64)
                               ).then_inc(z.s_act, 1)
                        z.act += 1

                @blk.gpsimd
                def _(g):
                    if grp == 1:
                        g.wait_ge(z.s_dw, z.dw)
                        g.dma_start(out=z.wo_sb.rearrange("d (h n) -> d h n", n=C),
                                    in_=z.wo_h.rearrange("(h d) n -> d h n", d=64)
                                    ).then_inc(z.s_dw, 16)
                        z.dw += 16
                        z.wo_done = z.dw

        _att_group(z, grp, kbuf, qbuf, wkq, vgrp, ptb, srow, rec_bc, wv, wv_done)


def _att_group(z, grp, kbuf, qbuf, wkq, vgrp, ptb, srow, rec_bc, wv, wv_done):
    nc = z.nc
    NHH = 8
    B = {nm: getattr(z, "c_" + nm) for nm in
         ("p_kq", "p_sa", "p_sr", "p_st", "p_ctx", "p_bc",
          "a_kq", "a_sr", "a_ex", "a_nc", "a_bc", "d_rd", "d_nm", "d_mu")}
    b_dsp = z.dsp
    dw0 = z.dw

    pf_targets = []
    for j in range(NHH):
        tgt = grp * 8 + j + 2
        if tgt < NH:
            pf_targets.append((j, tgt))
    cnt = dw0 + (16 if grp == 0 else 0)
    for (j, tgt) in pf_targets:
        cnt += 32
        z.wkq_cnt[tgt] = cnt
    pf_set = {j for (j, _) in pf_targets}


    def a_items(j):
        its = []
        if j < NHH:
            if j in pf_set:
                its.append(("pf", j))
            its += [("kq", j, g) for g in range(12)]
            its.append(("ks", j))
            for qt in range(NTO):
                its.append(("pa", j, qt))
                its.append(("sr", j, qt))
        return its

    def b_items(j):
        its = []
        for qh in range(2):
            for k in range(NKC):
                its.append(("st", j, qh, k))
                if k >= 1:
                    its.append(("cx", j, qh, k - 1))
            its.append(("cx", j, qh, NKC - 1))
            its.append(("nc", j, qh))
            if qh == 0:
                its += [("bc", j, 0, i) for i in range(2)]
                its.append(("mu", j, 0))
        return its

    regions = []
    for r in range(NHH + 2):
        prefix = []
        if r >= 2:
            prefix = [("bc", r - 2, 1, i) for i in range(2)] + [("mu", r - 2, 1)]
        A = a_items(r)
        Bl = b_items(r - 1) if 1 <= r <= NHH else []
        if not Bl:
            merged = prefix + A
        else:
            head_b, rest_b = Bl[:1], Bl[1:]
            merged = list(head_b) + prefix
            ai = 0
            for bi, bit in enumerate(rest_b):
                merged.append(bit)
                want = (bi + 1) * len(A) // len(rest_b)
                while ai < want:
                    merged.append(A[ai])
                    ai += 1
            merged += A[ai:]
        regions.append(merged)
    sched = [it for reg in regions for it in reg]
    pre_sr, pre_bc = {}, {}
    nsr = nbc = 0
    for i, it in enumerate(sched):
        if it[0] == "sr":
            pre_sr[i] = nsr; pre_bc[i] = nbc; nsr += 1
        elif it[0] == "bc":
            pre_sr[i] = nsr; pre_bc[i] = nbc; nbc += 1

    ks = z.ks_sb

    with ExitStack() as hs:
        aux = hs.enter_context(nc.psum_tensor(f"aux{grp}", [128, 1024], F32))
        STp = [hs.enter_context(nc.psum_tensor(f"ST{grp}{i}", [128, 1024], F32))
               for i in range(2)]
        ctx = hs.enter_context(nc.psum_tensor(f"ctx{grp}", [65, 1024], F32))

        with nc.Block() as blk:
            @blk.tensor
            def _(t):
                for isched, it in enumerate(sched):
                    kind = it[0]
                    if kind == "bc":
                        j, qh, i = it[1], it[2], it[3]
                        if i == 0:
                            t.wait_ge(z.a_nc, B["a_nc"] + 4 * j + 2 * (qh + 1))
                        if pre_sr[isched] > 0:
                            t.wait_ge(z.a_sr, B["a_sr"] + pre_sr[isched])
                        if pre_bc[isched] > 0:
                            t.wait_ge(z.a_bc, B["a_bc"] + pre_bc[isched])
                        t.matmul(aux[0:64, 512:1024], z.onesf[0:1, 0:64],
                                 srow[0:1, 512 * i: 512 * (i + 1)],
                                 start=True, stop=True).then_inc(z.p_bc, 1)
                        z.c_p_bc += 1
                    elif kind == "kq":
                        j, g = it[1], it[2]
                        h = grp * 8 + j
                        if g == 0:
                            t.wait_ge(z.s_dw, z.wkq_cnt[h])
                            if j >= 1:
                                t.wait_ge(z.d_rd, B["d_rd"] + 17 * j)
                        else:
                            t.wait_ge(z.a_kq, B["a_kq"] + 12 * j + g)
                        part = 0 if g < 8 else 1
                        ts = g if g < 8 else g - 8
                        for cc in range(NC_):
                            o64 = (1 - part) * 64
                            mm = t.matmul(aux[0:64, 0:512],
                                          wkq[h % 2][:, cc * 128 + o64: cc * 128 + o64 + 64],
                                          z.h_fm[:, cc * S + ts * 512: cc * S + (ts + 1) * 512],
                                          start=(cc == 0), stop=(cc == NC_ - 1))
                        mm.then_inc(z.p_kq, 1)
                        z.c_p_kq += 1
                    elif kind == "pa":
                        j, qt = it[1], it[2]
                        h = grp * 8 + j
                        if qt == 0:
                            t.wait_ge(z.d_rd, B["d_rd"] + 17 * j + 1)
                            t.wait_ge(z.a_kq, B["a_kq"] + 12 * (j + 1))
                        else:
                            t.wait_ge(z.d_rd, B["d_rd"] + 17 * j + qt + 1)
                        t.matmul(aux[:, 0:512],
                                 qbuf[h % 2][0:64, qt * 128:(qt + 1) * 128],
                                 ks[0:64, :],
                                 start=True, stop=True).then_inc(z.p_sa, 1)
                        z.c_p_sa += 1
                    elif kind == "sr":
                        j, qt = it[1], it[2]
                        t.wait_ge(z.d_rd, B["d_rd"] + 17 * j + qt + 2)
                        if pre_sr[isched] > 0:
                            t.wait_ge(z.a_sr, B["a_sr"] + pre_sr[isched])
                        if pre_bc[isched] > 0:
                            t.wait_ge(z.a_bc, B["a_bc"] + pre_bc[isched])
                        t.matmul(aux[0:1, 512:640],
                                 z.NM[:, qt:qt + 1], z.idf[:, :],
                                 start=True, stop=True).then_inc(z.p_sr, 1)
                        z.c_p_sr += 1
                    elif kind == "st":
                        j, qh, k = it[1], it[2], it[3]
                        h = grp * 8 + j
                        idx = 64 * j + 32 * qh + k
                        if idx >= 2:
                            t.wait_ge(z.a_ex, B["a_ex"] + idx - 1)
                        if qh == 0 and k == 0:
                            t.wait_ge(z.a_sr, B["a_sr"] + 16 * (j + 1))
                        for u in range(2):
                            mm = t.matmul(STp[k % 2][:, 512 * u:512 * (u + 1)],
                                          kbuf[h % 2][:, 128 * k:128 * (k + 1)],
                                          qbuf[h % 2][:, 1024 * qh + 512 * u: 1024 * qh + 512 * (u + 1)],
                                          start=True, stop=True)
                        mm.then_inc(z.p_st, 2)
                        z.c_p_st += 2
                    elif kind == "cx":
                        j, qh, k = it[1], it[2], it[3]
                        hh = j
                        t.wait_ge(z.a_ex, B["a_ex"] + 64 * j + 32 * qh + k + 1)
                        if k == 0:
                            w = B["d_mu"] + 2 * j + qh
                            if w > 0:
                                t.wait_ge(z.d_mu, w)
                        lhsT = vgrp[:, (k * 8 + hh) * 65:(k * 8 + hh) * 65 + 65]
                        for u in range(2):
                            mm = t.matmul(ctx[:, 512 * u:512 * (u + 1)],
                                          lhsT,
                                          ptb[k % 2][:, 512 * u: 512 * (u + 1)],
                                          start=(k == 0), stop=(k == NKC - 1))
                        mm.then_inc(z.p_ctx, 1)
                        z.c_p_ctx += 1

            @blk.scalar
            def _(s):
                for it in sched:
                    kind = it[0]
                    if kind == "st":
                        j, qh, k = it[1], it[2], it[3]
                        s.wait_ge(z.p_st, B["p_st"] + 128 * j + 2 * (32 * qh + k) + 2)
                        idx = 64 * j + 32 * qh + k
                        if idx >= 2:
                            s.wait_ge(z.p_ctx, B["p_ctx"] + idx - 1)
                        s.activation(ptb[k % 2][:, :],
                                     STp[k % 2][:, :],
                                     AF.Exp, scale=0.125).then_inc(z.a_ex, 1)
                        z.c_a_ex += 1
                    elif kind == "nc":
                        j, qh = it[1], it[2]
                        s.wait_ge(z.p_ctx, B["p_ctx"] + 64 * j + 32 * (qh + 1))
                        if B["p_bc"] + 4 * j + 2 * qh > 0:
                            s.wait_ge(z.p_bc, B["p_bc"] + 4 * j + 2 * qh)
                        sl = srow[0:1, 0:1024]
                        s.activation(sl, ctx[64:65, :], AF.Ln).then_inc(z.a_nc, 1)
                        z.c_a_nc += 1
                        s.wait_ge(z.a_nc, z.c_a_nc)
                        s.activation(sl, sl, AF.Exp, scale=-1.0).then_inc(z.a_nc, 1)
                        z.c_a_nc += 1

            @blk.vector
            def _(v):
                for it in sched:
                    kind = it[0]
                    if kind == "bc":
                        j, qh, i = it[1], it[2], it[3]
                        bcn = B["p_bc"] + 4 * j + 2 * qh + i
                        v.wait_ge(z.p_bc, bcn + 1)
                        if i == 0 and B["d_mu"] + 2 * j + qh > 0:
                            v.wait_ge(z.d_mu, B["d_mu"] + 2 * j + qh)
                        v.tensor_copy(rec_bc[:, 512 * i:512 * (i + 1)],
                                      aux[0:64, 512:1024]).then_inc(z.a_bc, 1)
                        z.c_a_bc += 1
                    elif kind == "kq":
                        j, g = it[1], it[2]
                        h = grp * 8 + j
                        v.wait_ge(z.p_kq, B["p_kq"] + 12 * j + g + 1)
                        if j >= 2:
                            v.wait_ge(z.p_st, B["p_st"] + 128 * (j - 1))
                            v.wait_ge(z.p_sa, B["p_sa"] + 16 * (j - 1))
                        part = 0 if g < 8 else 1
                        ts = g if g < 8 else g - 8
                        dst = (kbuf if part == 0 else qbuf)[h % 2][0:64, ts * 512:(ts + 1) * 512]
                        v.tensor_scalar(dst, aux[0:64, 0:512],
                                        z.bq_cols[:, (1 - part) * NH + h:
                                                  (1 - part) * NH + h + 1],
                                        None, ALU.add).then_inc(z.a_kq, 1)
                        z.c_a_kq += 1
                    elif kind == "sr":
                        j, qt = it[1], it[2]
                        h = grp * 8 + j
                        v.wait_ge(z.p_sr, B["p_sr"] + 16 * j + qt + 1)
                        v.tensor_copy(qbuf[h % 2][64:65, qt * 128:(qt + 1) * 128],
                                      aux[0:1, 512:640]).then_inc(z.a_sr, 1)
                        z.c_a_sr += 1
                    elif kind == "ks":
                        j = it[1]
                        h = grp * 8 + j
                        v.wait_ge(z.a_kq, B["a_kq"] + 12 * j + 8)
                        if j >= 1:
                            v.wait_ge(z.p_sa, B["p_sa"] + 16 * j)
                        v.tensor_copy(ks[:, :],
                                      kbuf[h % 2][0:64, :].rearrange(
                                          "p (n eight) -> p n eight", eight=8)[:, :, 0:1]
                                      ).then_inc(z.d_rd, 1)
                        z.c_d_rd += 1
                    elif kind == "pa":
                        j, qt = it[1], it[2]
                        v.wait_ge(z.p_sa, B["p_sa"] + 16 * j + qt + 1)
                        if j >= 2:
                            v.wait_ge(z.p_sr, B["p_sr"] + 16 * (j - 2) + qt + 1)
                        v.tensor_reduce(z.NM[:, qt:qt + 1], aux[:, 0:512],
                                        AX.X, ALU.max, negate=True
                                        ).then_inc(z.d_rd, 1)
                        z.c_d_rd += 1
                    elif kind == "mu":
                        j, qh = it[1], it[2]
                        h = grp * 8 + j
                        v.wait_ge(z.a_bc, B["a_bc"] + 4 * j + 2 * (qh + 1))
                        if 2 * j + qh >= 4:
                            v.wait_ge(z.s_dsp, b_dsp + 16 * (j - 1))
                        elif b_dsp > 0:
                            v.wait_ge(z.s_dsp, b_dsp)
                        v.tensor_mul(z.att_t[h % 2][:, 1024 * qh:1024 * (qh + 1)],
                                     ctx[0:64, :], rec_bc[:, 0:1024]).then_inc(z.d_mu, 1)
                        z.c_d_mu += 1

            @blk.sync
            def _(sy):
                for it in sched:
                    if it[0] == "mu" and it[2] == 1:
                        j = it[1]
                        h = grp * 8 + j
                        sy.wait_ge(z.d_mu, B["d_mu"] + 2 * j + 2)
                        sy.wait_ge(z.s_dsp, z.dsp)
                        sy.dma_start(
                            out=z.attn_scr.rearrange("(t d) c -> d t c", d=64)
                                [:, :, h * 128:(h + 1) * 128],
                            in_=z.att_t[h % 2][:, :].rearrange("d (t q) -> d t q", q=128)
                            ).then_inc(z.s_dsp, 16)
                        z.dsp += 16

            @blk.gpsimd
            def _(g):
                if grp == 0:
                    g.wait_ge(z.s_dw, z.dw)
                    g.dma_start(out=wv[0].rearrange("p (cc n) -> p cc n", n=512),
                                in_=z.wqkv_h[:, 2 * C + 512:3 * C].rearrange(
                                    "(cc p) n -> p cc n", p=128)).then_inc(z.s_dw, 16)
                    z.dw += 16
                    wv_done[1] = z.dw
                for it in sched:
                    if it[0] == "pf":
                        j = it[1]
                        tgt = grp * 8 + j + 2
                        g.wait_ge(z.p_kq, B["p_kq"] + 12 * (j + 1))
                        for half_ in range(2):
                            g.wait_ge(z.s_dw, z.dw)
                            g.dma_start(out=wkq[tgt % 2].rearrange("p (cc n) -> p cc n", n=128)
                                        [:, :, half_ * 64:half_ * 64 + 64],
                                        in_=z.wqkv_h[:, half_ * C + tgt * 64: half_ * C + tgt * 64 + 64]
                                        .rearrange("(cc p) n -> p cc n", p=128)).then_inc(z.s_dw, 16)
                            z.dw += 16
                        assert z.dw == z.wkq_cnt[tgt]


def _wo(z):
    nc = z.nc
    wo_sb = z.wo_sb
    apj_t = [z.sbm(f"apj_t{i}", [128, C], BF16) for i in range(2)]
    att_in = [z.sbm(f"att_in{i}", [64, NH * 128], BF16) for i in range(2)]
    z.w_in = z.sbt2("w_in", [128, NC_ * 2 * HID], BF16)
    z.w1s = z.sbt2("w1s", [128, NC_ * 2 * HID], BF16)
    with ExitStack() as ws:
        o_ps = [ws.enter_context(nc.psum_tensor(f"o_ps{i}", [128, C], F32)) for i in range(2)]

        b_pe, b_act, b_dsp, b_drl = z.pe, z.act, z.dsp, z.drl
        with nc.Block() as blk:
            @blk.gpsimd
            def _(g):
                g.wait_ge(z.s_dw, z.dw)
                g.dma_start(out=z.w_in.rearrange("p (cc n) -> p cc n", n=2 * HID),
                            in_=z.wsi_h.rearrange("(p cc) n -> p cc n", cc=NC_)
                            ).then_inc(z.s_dw, 16)
                z.dw += 16
                z.win_done = z.dw
                g.wait_ge(z.s_dw, z.dw)
                g.dma_start(out=z.w1s.rearrange("p (cc n) -> p cc n", n=2 * HID),
                            in_=z.w1_h.rearrange("(p cc) n -> p cc n", cc=NC_)
                            ).then_inc(z.s_dw, 16)
                z.dw += 16
                z.w1s_done = z.dw

            @blk.tensor
            def _(t):
                t.wait_ge(z.s_dw, z.wo_done)
                for tt in range(NTO):
                    p = tt % 2
                    if tt >= 2:
                        t.wait_ge(z.s_act, b_act + tt - 1)
                    t.wait_ge(z.s_drl, b_drl + (tt + 1) * 16)
                    for h in range(NH):
                        lhsT = att_in[p][0:64, h * 128:(h + 1) * 128]
                        rhs = wo_sb[0:64, h * C:(h + 1) * C]
                        for j in range(2):
                            mm = t.matmul(o_ps[p][:, j * 512:(j + 1) * 512], lhsT,
                                          rhs[:, j * 512:(j + 1) * 512],
                                          start=(h == 0), stop=(h == NH - 1))
                    mm.then_inc(z.s_pe, 1)
                    z.pe += 1

            @blk.scalar
            def _(s):
                for tt in range(NTO):
                    p = tt % 2
                    s.wait_ge(z.s_pe, b_pe + tt + 1)
                    if tt >= 2:
                        s.wait_ge(z.s_dsp, b_dsp + (tt - 2 + 1) * 16)
                    s.copy(apj_t[p][:, :], o_ps[p][:, :]).then_inc(z.s_act, 1)
                    z.act += 1

            @blk.sync
            def _(sy):
                sy.wait_ge(z.s_dsp, b_dsp)

                def _load(tt):
                    p = tt % 2
                    if tt >= 2:
                        sy.wait_ge(z.s_pe, b_pe + tt - 1)
                    sy.wait_ge(z.s_drl, z.drl)
                    sy.dma_start(out=att_in[p][:, :],
                                 in_=z.attn_scr[tt * 64:(tt + 1) * 64, :]
                                 ).then_inc(z.s_drl, 16)
                    z.drl += 16

                def _spill(tt):
                    p = tt % 2
                    sy.wait_ge(z.s_act, b_act + tt + 1)
                    sy.wait_ge(z.s_dsp, z.dsp)
                    sy.dma_start(out=z.apj_scr[tt * 128:(tt + 1) * 128, :],
                                 in_=apj_t[p][:, :]).then_inc(z.s_dsp, 16)
                    z.dsp += 16

                for tt in range(NTO):
                    _load(tt)
                    if tt >= 2:
                        _spill(tt - 2)
                _spill(NTO - 2)
                _spill(NTO - 1)


def _ffn(z):
    nc = z.nc
    w_in, w1s = z.w_in, z.w1s
    w_out = z.sbt("w_out", [128, NC_ * C], BF16)
    w2b = [z.sbt(f"w2b{i}", [128, NC_ * C], BF16) for i in range(2)]
    xnsl = [z.sbt(f"xnsl{i}", [128, NC_ * SL], BF16) for i in range(2)]
    swig = z.sbt("swig", [128, NC_ * SL], BF16)
    hid = z.sbt("hid", [128, NC_ * SL], BF16)
    hide = [z.sbt(f"hide{i}", [128, NC_ * SL], BF16) for i in range(2)]
    a_full = z.sbt("a_full", [128, NC_ * SL], BF16)
    bc_all = z.sbt("bc_all", [128, E * SL], BF16)
    wdT = z.sbt("wdT", [1, E * 128], BF16)
    g_sc = z.sbt("g_sc", [128, 116], F32)
    rs_t = [z.sbt(f"f_rs{i}", [128, C], F32) for i in range(2)]
    ap_t = [z.sbt(f"f_ap{i}", [128, C], BF16) for i in range(2)]
    o_t = [z.sbt(f"f_o{i}", [128, C], F32) for i in range(2)]

    w2_done = [0, 0]
    xnsl_done = [0, 0]

    with nc.Block() as blk:
        @blk.gpsimd
        def _(g):
            g.wait_ge(z.s_dw, z.dw)
            g.dma_start(out=w_out.rearrange("p (cc n) -> p cc n", n=C),
                        in_=z.wso_h.rearrange("(p cc) n -> p cc n", cc=NC_)).then_inc(z.s_dw, 16)
            z.dw += 16
            z.ffn_w_done = z.dw
            g.wait_ge(z.s_dw, z.dw)
            g.dma_start(out=w2b[0].rearrange("p (cc n) -> p cc n", n=C),
                        in_=z.w2_h[0].rearrange("(p cc) n -> p cc n", cc=NC_)).then_inc(z.s_dw, 16)
            z.dw += 16
            w2_done[0] = z.dw

        @blk.sync
        def _(sy):
            sy.wait_ge(z.s_dsp, z.dsp)
            sy.wait_ge(z.s_drl, z.drl)
            sy.dma_start(out=xnsl[0].rearrange("p (cc n) -> p cc n", n=SL),
                         in_=z.xnfm_scr[0:128 * NC_, :].rearrange(
                             "(p cc) n -> p cc n", cc=NC_)).then_inc(z.s_drl, 16)
            z.drl += 16
            xnsl_done[0] = z.drl

    for sl in range(NSL):
        slb = sl % 2
        t0 = sl * SL
        with ExitStack() as fs:
          with ExitStack() as fs1:
            z_ps = [fs1.enter_context(nc.psum_tensor(f"z_ps{sl}{i}", [128, SL], F32))
                    for i in range(2)]

            for rnd, (wmat, dest) in enumerate(((w_in, swig), (w1s, hid))):
                b_pe, b_act, b_dve = z.pe, z.act, z.dve
                with nc.Block() as blk:
                    @blk.tensor
                    def _(t):
                        t.wait_ge(z.s_dw, z.win_done if rnd == 0 else z.w1s_done)
                        t.wait_ge(z.s_drl, xnsl_done[slb])
                        for j in range(16):
                            fp = j % 2
                            if j >= 2:
                                prev = j - 2
                                if prev < 8:
                                    t.wait_ge(z.s_act, b_act + prev + 1)
                                else:
                                    t.wait_ge(z.s_dve, b_dve + (prev - 8) + 1)
                            for cc in range(NC_):
                                mm = t.matmul(z_ps[fp][:, :],
                                              wmat[:, cc * 2 * HID + j * 128: cc * 2 * HID + (j + 1) * 128],
                                              xnsl[slb][:, cc * SL:(cc + 1) * SL],
                                              start=(cc == 0), stop=(cc == NC_ - 1))
                            mm.then_inc(z.s_pe, 1)
                            z.pe += 1

                    @blk.scalar
                    def _(s):
                        for j in range(8):
                            s.wait_ge(z.s_pe, b_pe + j + 1)
                            s.activation(a_full[:, j * SL:(j + 1) * SL], z_ps[j % 2][:, :],
                                         AF.Silu).then_inc(z.s_act, 1)
                            z.act += 1

                    @blk.vector
                    def _(v):
                        for j in range(8, 16):
                            v.wait_ge(z.s_pe, b_pe + j + 1)
                            v.wait_ge(z.s_act, b_act + (j - 8) + 1)
                            v.tensor_mul(dest[:, (j - 8) * SL:(j - 7) * SL],
                                         z_ps[j % 2][:, :], a_full[:, (j - 8) * SL:(j - 7) * SL]
                                         ).then_inc(z.s_dve, 1)
                            z.dve += 1

                    if rnd == 1:
                        @blk.sync
                        def _(sy):
                            if sl < NSL - 1:
                                sy.wait_ge(z.s_drl, z.drl)
                                sy.dma_start(
                                    out=xnsl[1 - slb].rearrange("p (cc n) -> p cc n", n=SL),
                                    in_=z.xnfm_scr[(sl + 1) * 128 * NC_:(sl + 2) * 128 * NC_, :]
                                    .rearrange("(p cc) n -> p cc n", cc=NC_)
                                    ).then_inc(z.s_drl, 16)
                                z.drl += 16
                                xnsl_done[1 - slb] = z.drl

            _gate(z, fs1, sl, slb, xnsl, wdT, bc_all, g_sc)

          _experts(z, fs, sl, slb, hid, hide, swig, bc_all, w2b, w_out,
                   rs_t, ap_t, o_t, t0, w2_done)


def _gate(z, fs, sl, slb, xnsl, wdT, bc_all, g_sc):
    nc = z.nc
    g_ps = fs.enter_context(nc.psum_tensor(f"g_ps{sl}", [128, 32], F32))
    wdt_ps = fs.enter_context(nc.psum_tensor(f"wdt_ps{sl}", [1, E * 128], F32))
    bc_ps = fs.enter_context(nc.psum_tensor(f"bcg_ps{sl}", [128, E * 128], F32))
    b_pe2, b_act2, b_pool, b_dve = z.pe2, z.act2, z.pool, z.dve
    GA = g_sc[:, 0:32].rearrange("p (t e) -> p t e", e=8)
    GE = g_sc[:, 32:64].rearrange("p (t e) -> p t e", e=8)
    GM = g_sc[:, 64:96].rearrange("p (t e) -> p t e", e=8)
    c_nm = g_sc[:, 96:100]
    c_sm = g_sc[:, 100:104]
    c_rc = g_sc[:, 104:108]
    c_m1 = g_sc[:, 108:112]
    c_m2 = g_sc[:, 112:116]

    with nc.Block() as blk:
        @blk.tensor
        def _(t):
            for tt in range(4):
                for cc in range(NC_):
                    mm = t.matmul(g_ps[:, tt * 8:(tt + 1) * 8],
                                  xnsl[slb][:, cc * SL + tt * 128: cc * SL + (tt + 1) * 128],
                                  z.wgate[:, cc * E:(cc + 1) * E],
                                  start=(cc == 0), stop=(cc == NC_ - 1))
            mm.then_inc(z.s_pe2, 1)
            z.pe2 += 1
            t.wait_ge(z.s_pool, b_pool + 1)
            for tt in range(4):
                if tt >= 1:
                    t.wait_ge(z.s_act2, b_act2 + 2 * tt)
                for e in range(E):
                    mm = t.matmul(wdt_ps[0:1, e * 128:(e + 1) * 128],
                                  GM[:, tt, e:e + 1], z.idf[:, :],
                                  start=True, stop=True)
                mm.then_inc(z.s_pe2, 1)
                z.pe2 += 1
                t.wait_ge(z.s_act2, b_act2 + 2 * tt + 2)
                if tt >= 1:
                    t.wait_ge(z.s_act2, b_act2 + 2 * tt + 1)
                for e in range(E):
                    mm = t.matmul(bc_ps[:, e * 128:(e + 1) * 128], z.ones_bf[0:1, 0:128],
                                  wdT[0:1, e * 128:(e + 1) * 128], start=True, stop=True)
                mm.then_inc(z.s_pe2, 1)
                z.pe2 += 1

        @blk.vector
        def _(v):
            dv = z.dve
            v.wait_ge(z.s_pe2, b_pe2 + 1)
            v.tensor_add(GA, g_ps[:, :].rearrange("p (t e) -> p t e", e=8),
                         z.gb_bc[:, :].unsqueeze(1).broadcast_to([128, 4, 8])
                         ).then_inc(z.s_dve, 1)
            dv += 1
            v.wait_ge(z.s_dve, dv)
            v.tensor_reduce(c_nm, GA, AX.X, ALU.max, negate=True).then_inc(z.s_dve, 1)
            dv += 1
            v.wait_ge(z.s_dve, dv)
            v.tensor_tensor(g_sc[:, 32:64].rearrange("p (t e) -> p t e", e=8), g_sc[:, 0:32].rearrange("p (t e) -> p t e", e=8),
                            c_nm.unsqueeze(2).broadcast_to([128, 4, 8]),
                            ALU.add).then_inc(z.s_dve, 1)
            dv += 1
            v.wait_ge(z.s_act2, b_act2 + 1)
            v.tensor_reduce(c_sm, GE, AX.X, ALU.add).then_inc(z.s_dve, 1)
            dv += 1
            v.wait_ge(z.s_dve, dv)
            v.reciprocal(c_rc, c_sm).then_inc(z.s_dve, 1)
            dv += 1
            v.wait_ge(z.s_dve, dv)
            v.tensor_reduce(c_m1, GE, AX.X, ALU.max).then_inc(z.s_dve, 1)
            dv += 1
            v.wait_ge(z.s_dve, dv)
            v.tensor_tensor(g_sc[:, 64:96].rearrange("p (t e) -> p t e", e=8), g_sc[:, 32:64].rearrange("p (t e) -> p t e", e=8),
                            c_m1.unsqueeze(2).broadcast_to([128, 4, 8]),
                            ALU.is_ge).then_inc(z.s_dve, 1)
            dv += 1
            v.wait_ge(z.s_dve, dv)
            v.scalar_tensor_tensor(g_sc[:, 64:96], g_sc[:, 64:96], -1e30,
                                   g_sc[:, 32:64], ALU.mult, ALU.add).then_inc(z.s_dve, 1)
            dv += 1
            v.wait_ge(z.s_dve, dv)
            v.tensor_reduce(c_m2, GM, AX.X, ALU.max).then_inc(z.s_dve, 1)
            dv += 1
            v.wait_ge(z.s_dve, dv)
            v.tensor_tensor(g_sc[:, 64:96].rearrange("p (t e) -> p t e", e=8), g_sc[:, 32:64].rearrange("p (t e) -> p t e", e=8),
                            c_m2.unsqueeze(2).broadcast_to([128, 4, 8]),
                            ALU.is_ge).then_inc(z.s_dve, 1)
            dv += 1
            v.wait_ge(z.s_dve, dv)
            v.tensor_mul(g_sc[:, 64:96], g_sc[:, 64:96],
                         g_sc[:, 32:64]).then_inc(z.s_dve, 1)
            dv += 1
            v.wait_ge(z.s_dve, dv)
            v.tensor_tensor(g_sc[:, 64:96].rearrange("p (t e) -> p t e", e=8), g_sc[:, 64:96].rearrange("p (t e) -> p t e", e=8),
                            c_rc.unsqueeze(2).broadcast_to([128, 4, 8]),
                            ALU.mult).then_inc(z.s_pool, 1)
            z.pool += 1
            z.dve = dv

        @blk.scalar
        def _(s):
            s.wait_ge(z.s_dve, b_dve + 3)
            s.activation(g_sc[:, 32:64], g_sc[:, 32:64], AF.Exp,
                         scale=1.0).then_inc(z.s_act2, 1)
            z.act2 += 1
            for tt in range(4):
                s.wait_ge(z.s_pe2, b_pe2 + 2 * tt + 2)
                if tt >= 1:
                    s.wait_ge(z.s_pe2, b_pe2 + 2 * tt + 1)
                s.copy(wdT[0:1, :], wdt_ps[0:1, :]).then_inc(z.s_act2, 1)
                z.act2 += 1
                s.wait_ge(z.s_pe2, b_pe2 + 2 * tt + 3)
                s.copy(bc_all.rearrange("p (e n) -> p e n", n=SL)[:, :, tt * 128:(tt + 1) * 128],
                       bc_ps.rearrange("p (e n) -> p e n", n=128)).then_inc(z.s_act2, 1)
                z.act2 += 1


def _experts(z, fs, sl, slb, hid, hide, swig, bc_all, w2b, w_out,
             rs_t, ap_t, o_t, t0, w2_done):
    nc = z.nc
    out_ps = [fs.enter_context(nc.psum_tensor(f"out_ps{sl}{i}", [128, C], F32))
              for i in range(4)]
    b_pe, b_pe2, b_dve, b_dve2 = z.pe, z.pe2, z.dve, z.dve2
    b_drl, b_dout, b_dw = z.drl, z.dout, z.dw
    w2_cnt = [w2_done[0]] + [b_dw + e * 16 for e in range(1, E)]

    with nc.Block() as blk:
        @blk.vector
        def _(v):
            for e in range(E):
                if e >= 2:
                    v.wait_ge(z.s_pe, b_pe + 1 + (e - 1) * 4)
                for cc in range(NC_):
                    tm = v.tensor_mul(hide[e % 2][:, cc * SL:(cc + 1) * SL],
                                      hid[:, cc * SL:(cc + 1) * SL],
                                      bc_all[:, e * SL:(e + 1) * SL])
                tm.then_inc(z.s_dve, 1)
                z.dve += 1
            dv2 = b_dve2
            for tt in range(4):
                p = tt % 2
                v.wait_ge(z.s_pe, b_pe + 1 + (E - 1) * 4 + tt + 1)
                v.wait_ge(z.s_drl, b_drl + (2 * tt + 2) * 16)
                if b_dout + (tt - 1) * 16 > 0:
                    v.wait_ge(z.s_dout, b_dout + (tt - 1) * 16)
                v.scalar_tensor_tensor(o_t[p][:, :], out_ps[tt][:, :], 1.0,
                                       rs_t[p][:, :], ALU.mult, ALU.add).then_inc(z.s_dve2, 1)
                dv2 += 1
                v.wait_ge(z.s_dve2, dv2)
                v.tensor_add(o_t[p][:, :], o_t[p][:, :], z.bo_bc[:, :]).then_inc(z.s_dve2, 1)
                dv2 += 1
                v.wait_ge(z.s_dve2, dv2)
                v.tensor_add(o_t[p][:, :], o_t[p][:, :], ap_t[p][:, :]).then_inc(z.s_dve2, 1)
                dv2 += 1
            z.dve2 = dv2

        @blk.tensor
        def _(t):
            t.wait_ge(z.s_dw, z.ffn_w_done)
            for tt in range(4):
                for cc in range(NC_):
                    for j in range(2):
                        mm = t.matmul(out_ps[tt][:, j * 512:(j + 1) * 512],
                                      swig[:, cc * SL + tt * 128: cc * SL + (tt + 1) * 128],
                                      w_out[:, cc * C + j * 512: cc * C + j * 512 + 512],
                                      start=(cc == 0), stop=False)
            mm.then_inc(z.s_pe, 1)
            z.pe += 1
            for e in range(E):
                t.wait_ge(z.s_dve, b_dve + e + 1)
                t.wait_ge(z.s_dw, w2_cnt[e])
                for tt in range(4):
                    for cc in range(NC_):
                        for j in range(2):
                            mm = t.matmul(out_ps[tt][:, j * 512:(j + 1) * 512],
                                          hide[e % 2][:, cc * SL + tt * 128: cc * SL + (tt + 1) * 128],
                                          w2b[e % 2][:, cc * C + j * 512: cc * C + j * 512 + 512],
                                          start=False,
                                          stop=(e == E - 1 and cc == NC_ - 1))
                    mm.then_inc(z.s_pe, 1)
                    z.pe += 1

        @blk.gpsimd
        def _(g):
            g.wait_ge(z.s_dsp, z.dsp)
            for e in range(1, E):
                g.wait_ge(z.s_pe, b_pe + 1 + max(0, e - 1) * 4)
                g.wait_ge(z.s_dw, z.dw)
                g.dma_start(out=w2b[e % 2].rearrange("p (cc n) -> p cc n", n=C),
                            in_=z.w2_h[e].rearrange("(p cc) n -> p cc n", cc=NC_)
                            ).then_inc(z.s_dw, 16)
                z.dw += 16
                assert z.dw == w2_cnt[e]
            if sl < NSL - 1:
                g.wait_ge(z.s_pe, b_pe + 1 + 7 * 4)
                g.wait_ge(z.s_dw, z.dw)
                g.dma_start(out=w2b[0].rearrange("p (cc n) -> p cc n", n=C),
                            in_=z.w2_h[0].rearrange("(p cc) n -> p cc n", cc=NC_)
                            ).then_inc(z.s_dw, 16)
                z.dw += 16
                w2_done[0] = z.dw

        @blk.sync
        def _(sy):
            sy.wait_ge(z.s_dsp, z.dsp)
            sy.wait_ge(z.s_dsp2, z.dsp2)
            for tt in range(4):
                p = tt % 2
                if b_dve2 + (tt - 2) * 3 + 1 > 0:
                    sy.wait_ge(z.s_dve2, b_dve2 + (tt - 2) * 3 + 1)
                sy.wait_ge(z.s_drl, z.drl)
                sy.dma_start(out=rs_t[p][:, :],
                             in_=z.resid_scr[t0 + tt * 128: t0 + (tt + 1) * 128, :]
                             ).then_inc(z.s_drl, 16)
                z.drl += 16
                if b_dve2 + (tt - 2) * 3 + 3 > 0:
                    sy.wait_ge(z.s_dve2, b_dve2 + (tt - 2) * 3 + 3)
                sy.wait_ge(z.s_drl, z.drl)
                sy.dma_start(out=ap_t[p][:, :],
                             in_=z.apj_scr[t0 + tt * 128: t0 + (tt + 1) * 128, :]
                             ).then_inc(z.s_drl, 16)
                z.drl += 16
            for tt in range(4):
                p = tt % 2
                sy.wait_ge(z.s_dve2, b_dve2 + tt * 3 + 3)
                sy.wait_ge(z.s_dout, z.dout)
                sy.dma_start(out=z.out_h[t0 + tt * 128: t0 + (tt + 1) * 128, :],
                             in_=o_t[p][:, :]).then_inc(z.s_dout, 16)
                z.dout += 16


def _early_out(z):
    nc = z.nc
    buf = z.sbt("early", [128, C], F32)
    with nc.Block() as blk:
        @blk.sync
        def _(sy):
            for t in range(NTO):
                sy.wait_ge(z.s_drl, z.drl)
                sy.dma_start(out=buf[:, :],
                             in_=z.resid_scr[t * 128:(t + 1) * 128, :]).then_inc(z.s_drl, 16)
                z.drl += 16
                sy.wait_ge(z.s_drl, z.drl)
                sy.wait_ge(z.s_dout, z.dout)
                sy.dma_start(out=z.out_h[t * 128:(t + 1) * 128, :],
                             in_=buf[:, :]).then_inc(z.s_dout, 16)
                z.dout += 16
                sy.wait_ge(z.s_dout, z.dout)


def kernel(L0_attn_x, L0_attn_gn_gamma, L0_attn_gn_beta, L0_attn_W_qkv,
           L0_attn_b_qkv, L0_attn_W_o, L0_attn_b_o, L0_ffn_w_shared_in,
           L0_ffn_w_shared_out, L0_ffn_w1_shared, L0_ffn_w2, L0_ffn_w_gate,
           L0_ffn_gate_bias):
    bf = ml_dtypes.bfloat16
    x = np.ascontiguousarray(np.asarray(L0_attn_x, dtype=np.float32))
    nc = build_program()

    def permrows(w):
        cc = w.shape[0] // 128
        return np.ascontiguousarray(
            w.reshape(cc, 128, w.shape[-1]).transpose(1, 0, 2).reshape(w.shape))

    common = {
        "wqkv": np.ascontiguousarray(L0_attn_W_qkv, dtype=np.float32),
        "bqkv": np.ascontiguousarray(L0_attn_b_qkv, dtype=np.float32).reshape(1, 3 * C),
        "bq_cols": np.ascontiguousarray(np.concatenate(
            [np.asarray(L0_attn_b_qkv, dtype=np.float32).reshape(3 * C)[p * C:p * C + C]
             .reshape(NH, 64).T for p in range(2)], axis=1)),
        "wo": np.ascontiguousarray(L0_attn_W_o, dtype=np.float32),
        "bo": np.ascontiguousarray(L0_attn_b_o, dtype=np.float32).reshape(1, C),
        "gamma_cols": np.ascontiguousarray(
            np.asarray(L0_attn_gn_gamma, dtype=np.float32).reshape(NC_, 128).T),
        "beta_cols": np.ascontiguousarray(
            np.asarray(L0_attn_gn_beta, dtype=np.float32).reshape(NC_, 128).T),
        "w_sh_in": permrows(np.asarray(L0_ffn_w_shared_in, dtype=np.float32)),
        "w_sh_out": permrows(np.asarray(L0_ffn_w_shared_out, dtype=np.float32)),
        "w1_sh": permrows(np.asarray(L0_ffn_w1_shared, dtype=np.float32)),
        "w2": np.stack([permrows(np.asarray(L0_ffn_w2[e], dtype=np.float32))
                        for e in range(E)]),
        "w_gate": np.ascontiguousarray(L0_ffn_w_gate, dtype=np.float32),
        "gate_bias": np.ascontiguousarray(L0_ffn_gate_bias, dtype=np.float32).reshape(1, E),
        "identf": np.eye(128, dtype=np.float32),
        "identb": np.eye(128, dtype=bf),
        "ind4": np.ascontiguousarray(np.repeat(np.eye(4, dtype=np.float32), 32, axis=0)),
        "ind4t": np.ascontiguousarray(np.repeat(np.eye(4, dtype=np.float32), 32, axis=0).T),
        "onesf": np.ones((1, 128), dtype=np.float32),
    }
    in_maps = []
    for c in range(8):
        b, m = c // 2, c % 2
        xb = x[b]
        xp = np.concatenate([xb[m * HALF:(m + 1) * HALF], xb[(1 - m) * HALF:(2 - m) * HALF]])
        d = dict(common)
        d["x"] = np.ascontiguousarray(xp)
        in_maps.append(d)

    res = run_bass_kernel_spmd(nc, in_maps, core_ids=list(range(8)))
    out = np.empty((B, S, C), dtype=np.float32)
    for c in range(8):
        b, m = c // 2, c % 2
        out[b, m * HALF:(m + 1) * HALF, :] = res.results[c]["out"]
    return out
